# version 30
# baseline (speedup 1.0000x reference)
import numpy as np
from contextlib import ExitStack

import concourse.bass as bass
import concourse.mybir as mybir
from concourse.bass_utils import run_bass_kernel_spmd

F32 = mybir.dt.float32
BF16 = mybir.dt.bfloat16
AF = mybir.ActivationFunctionType
ALU = mybir.AluOpType
AX = mybir.AxisListType

ENGS = ("sp", "pe", "act", "dve", "pool")

D = 1024
DC = 8
I = 2816
SEQ = 2
LN_EPS = 1e-5

QTH = (0.25, 0.75, 1.25, 1.75, 2.5, 3.5, 5.0)
QDL = (0.5, 0.5, 0.5, 0.5, 1.0, 1.0, 2.0)

WQW_G = 704
WQ_SLOTS = 15


def wq_unit_list():
    units = []
    for ic in range(I // 128):
        units.append(("g", ic * 128, 0, D // 2))
        units.append(("g", ic * 128, D // 2, D // 2))
    for ic in range(I // 128):
        units.append(("u", ic * 128, 0, D // 2))
        units.append(("u", ic * 128, D // 2, D // 2))
    for dc in range(DC):
        for q4 in range(4):
            units.append(("d", dc * 128, q4 * (I // 4), I // 4))
    return units


class Buf:
    __slots__ = ("name", "last_w", "readers", "dma_cnt")

    def __init__(self, name):
        self.name = name
        self.last_w = None
        self.readers = {}
        self.dma_cnt = {}


class Sched:

    def __init__(self):
        self.items = {e: [] for e in ENGS}
        self.cnt = {e: 0 for e in ENGS}
        self.pending_fence = {}
        self.sem_names = list(ENGS)
        self.dmatot = {}

    def fence_all(self):
        snap = [(e, self.cnt[e]) for e in ENGS] + list(self.dmatot.items())
        for e in ENGS:
            self.pending_fence[e] = list(snap)

    def step(self, eng, fn, reads=(), writes=(), dma=False, extra_deps=()):
        deps = list(extra_deps)
        if eng in self.pending_fence:
            deps.extend(self.pending_fence.pop(eng))
        for b in reads:
            deps.append(b.last_w)
        for b in writes:
            deps.append(b.last_w)
            deps.extend(b.readers.items())
        waits = {}
        for d in deps:
            if d is None:
                continue
            de, dv = d
            if eng == "pe" and de == "pe":
                continue
            if dv > 0 and waits.get(de, -1) < dv:
                waits[de] = dv
        if dma:
            assert len(writes) == 1, "DMA must write exactly one Buf"
            wb = writes[0]
            sem = ("dmasw_" if eng == "pool" else "dma_") + wb.name
            if sem not in self.sem_names:
                self.sem_names.append(sem)
            wb.dma_cnt[sem] = wb.dma_cnt.get(sem, 0) + 1
            self.items[eng].append((fn, waits, (sem, 16)))
            tok = (sem, 16 * wb.dma_cnt[sem])
            self.dmatot[sem] = tok[1]
        else:
            self.cnt[eng] += 1
            self.items[eng].append((fn, waits, (eng, 1)))
            tok = (eng, self.cnt[eng])
        for b in reads:
            cur = b.readers.get(tok[0], -1)
            if cur < tok[1]:
                b.readers[tok[0]] = tok[1]
        for b in writes:
            b.last_w = tok
            b.readers = {}
        return tok


def ceil_div(a, b):
    return (a + b - 1) // b


def build(T=2048, TC=256):
    NT = SEQ * T
    NI = SEQ * (T // TC)
    NP2 = NT // 128
    ipcs = []
    p = 0
    while p < I:
        w = min(512, I - p)
        ipcs.append((p, w))
        p += w

    nc = bass.Bass("TRN2", target_bir_lowering=False)

    x_in = nc.dram_tensor("x", [NT, D], F32, kind="ExternalInput").ap()
    wkT = nc.dram_tensor("wkT", [D, D], F32, kind="ExternalInput").ap()
    wvT = nc.dram_tensor("wvT", [D, D], F32, kind="ExternalInput").ap()
    wrT = nc.dram_tensor("wrT", [D, D], F32, kind="ExternalInput").ap()
    woT = nc.dram_tensor("woT", [D, D], F32, kind="ExternalInput").ap()
    wg_in = nc.dram_tensor("wg", [I, D], F32, kind="ExternalInput").ap()
    wu_in = nc.dram_tensor("wu", [I, D], F32, kind="ExternalInput").ap()
    wd_in = nc.dram_tensor("wd", [D, I], F32, kind="ExternalInput").ap()
    td2 = nc.dram_tensor("td2", [128, DC], F32, kind="ExternalInput").ap()
    tf2 = nc.dram_tensor("tf2", [128, DC], F32, kind="ExternalInput").ap()
    tmk2 = nc.dram_tensor("tmk2", [128, DC], F32, kind="ExternalInput").ap()
    tmv2 = nc.dram_tensor("tmv2", [128, DC], F32, kind="ExternalInput").ap()
    tmr2 = nc.dram_tensor("tmr2", [128, DC], F32, kind="ExternalInput").ap()
    ident_in = nc.dram_tensor("ident", [128, 128], BF16, kind="ExternalInput").ap()
    out = nc.dram_tensor("out", [NT, D], F32, kind="ExternalOutput").ap()

    xscr = nc.dram_tensor("xscr", [SEQ, T + 1, D], BF16, kind="Internal").ap()
    zscr = nc.dram_tensor("zscr", [NI, 128, DC * TC], BF16, kind="Internal").ap()
    wgq_scr = nc.dram_tensor("wgq_scr", [I, D], BF16, kind="Internal").ap()
    wuq_scr = nc.dram_tensor("wuq_scr", [I, D], BF16, kind="Internal").ap()
    wdq_scr = nc.dram_tensor("wdq_scr", [D, I], BF16, kind="Internal").ap()


    sch = Sched()
    S = sch.step
    B = lambda n: Buf(n)

    es0 = ExitStack()
    sbt0 = lambda n, s, d: es0.enter_context(nc.sbuf_tensor(n, s, d))
    idn = sbt0("idn", [128, 128], BF16)
    b_idn = B("idn")
    b_xscr, b_zscr = B("xscr"), B("zscr")
    b_wqscr = {"g": B("wgscr"), "u": B("wuscr"), "d": B("wdscr")}
    b_out = B("out")

    with es0:
        sems = {}
        sem_es = ExitStack()
        with sem_es:
            for e in ENGS:
                sems[e] = sem_es.enter_context(nc.semaphore(f"sem_{e}"))

            es1 = ExitStack()
            sbt = lambda n, s, d: es1.enter_context(nc.sbuf_tensor(n, s, d))
            pst = lambda n, s, d: es1.enter_context(nc.psum_tensor(n, s, d))

            wk_sb = sbt("wk_sb", [128, DC * D], BF16)
            wv_sb = sbt("wv_sb", [128, DC * D], BF16)
            wr_sb = sbt("wr_sb", [128, DC * D], BF16)
            wo_sb = sbt("wo_sb", [128, DC * D], BF16)
            tdc = sbt("tdc", [128, DC], F32)
            ewc = sbt("ewc", [128, DC], F32)
            ufc = sbt("ufc", [128, DC], F32)
            tmkc = sbt("tmkc", [128, DC], F32)
            tmvc = sbt("tmvc", [128, DC], F32)
            tmrc = sbt("tmrc", [128, DC], F32)
            zrow = sbt("zrow", [1, D], BF16)

            xa = [sbt(f"xa{j}", [128, DC * TC], BF16) for j in range(3)]
            xb = [sbt(f"xb{j}", [128, DC * TC], BF16) for j in range(3)]
            _dxb = sbt("dxb", [128, DC * TC], BF16)
            dxb = [_dxb, _dxb]
            mixk = [sbt(f"mixk{j}", [128, DC * TC], BF16) for j in range(2)]
            mixv = [sbt(f"mixv{j}", [128, DC * TC], BF16) for j in range(2)]
            mixr = [sbt(f"mixr{j}", [128, DC * TC], BF16) for j in range(2)]
            _mtmp = sbt("mtmp", [128, DC * TC], BF16)
            mtmp = [_mtmp, _mtmp]
            ek_t = sbt("ek_t", [128, DC * TC], BF16)
            euk_t = sbt("euk_t", [128, DC * TC], BF16)
            vv_t = sbt("vv_t", [128, DC * TC], BF16)
            ekv_t = sbt("ekv_t", [128, DC * TC], BF16)
            eukv_t = sbt("eukv_t", [128, DC * TC], BF16)
            sr_t = sbt("sr_t", [128, DC * TC], BF16)
            er_t = [sbt(f"er_t{j}", [128, TC], BF16) for j in range(2)]
            l1_t = [sbt(f"l1_t{j}", [128, TC], F32) for j in range(2)]
            rwkv_t = [sbt(f"rwkv{j}", [128, DC * TC], BF16) for j in range(2)]
            _z16 = sbt("z16", [128, DC * TC], BF16)
            z16_t = [_z16, _z16]
            alfp = sbt("alfp", [128, 2 * (TC + 1)], F32)
            betp = sbt("betp", [128, 2 * (TC + 1)], F32)
            alf = [alfp[:, 0:TC + 1], alfp[:, TC + 1:2 * (TC + 1)]]
            bet = [betp[:, 0:TC + 1], betp[:, TC + 1:2 * (TC + 1)]]
            numt = [sbt(f"numt{j}", [128, 2 * TC], F32) for j in range(2)]
            dent = [sbt(f"dent{j}", [128, 2 * TC], F32) for j in range(2)]
            rden = [sbt(f"rden{j}", [128, 2 * TC], BF16) for j in range(2)]
            wkvt = [sbt(f"wkvt{j}", [128, 2 * TC], BF16) for j in range(2)]
            acar = sbt("acar", [128, DC], F32)
            bcar = sbt("bcar", [128, DC], F32)

            WQW = WQW_G
            wload = [sbt(f"wload{j}", [128, WQW], F32) for j in range(2)]
            _wqa = sbt("wq_a", [128, WQW], F32)
            wq_a = [_wqa, _wqa]
            wq_t = [sbt(f"wq_t{j}", [128, WQW], F32) for j in range(2)]
            _wqc3 = [sbt(f"wq_c{k}", [128, WQW], BF16) for k in range(3)]
            wq_f = [sbt(f"wq_f{j}", [128, WQW], BF16) for j in range(2)]
            wq_c = [_wqc3 + [wq_f[0]], _wqc3 + [wq_f[1]]]
            _wqsg = sbt("wq_sg", [128, WQW], BF16)
            wq_sg = [_wqsg, _wqsg]
            wq_bm = [sbt(f"wq_bm{j}", [128, WQW // 16], F32) for j in range(2)]
            wq_rs = [sbt(f"wq_rs{j}", [128, 2 * (WQW // 16)], F32) for j in range(2)]
            wq_o = [sbt(f"wq_o{j}", [128, WQW], BF16) for j in range(2)]

            ps_k = [pst(f"ps_k{j}", [128, TC], F32) for j in range(2)]
            ps_v = [pst(f"ps_v{j}", [128, TC], F32) for j in range(2)]
            ps_r = [pst(f"ps_r{j}", [128, TC], F32) for j in range(2)]
            ps_at = [pst(f"ps_at{j}", [128, TC], F32) for j in range(2)]

            b_wk, b_wv, b_wr, b_wo = B("wk"), B("wv"), B("wr"), B("wo")
            b_tdc, b_ewc, b_ufc = B("tdc"), B("ewc"), B("ufc")
            b_tmk, b_tmv, b_tmr = B("tmk"), B("tmv"), B("tmr")
            b_zrow = B("zrow")
            b_xa = [B(f"xa{j}") for j in range(3)]
            b_xb = [B(f"xb{j}") for j in range(3)]
            _bdxb = B("dxb")
            b_dxb = [_bdxb, _bdxb]
            b_mixk = [B(f"mixk{j}") for j in range(2)]
            b_mixv = [B(f"mixv{j}") for j in range(2)]
            b_mixr = [B(f"mixr{j}") for j in range(2)]
            b_mtmp4 = [B(f"mtmp{j}") for j in range(4)]
            b_ek, b_euk, b_vv = B("ek"), B("euk"), B("vv")
            b_ekv, b_eukv = B("ekv"), B("eukv")
            b_sr = B("sr")
            b_er = [B(f"er{j}") for j in range(2)]
            b_l1 = [B(f"l1{j}") for j in range(2)]
            b_rwkv = [B(f"rwkv{j}") for j in range(2)]
            _bz16 = B("z16")
            b_z16 = [_bz16, _bz16]
            b_alf = [B(f"alf{j}") for j in range(2)]
            b_bet = [B(f"bet{j}") for j in range(2)]
            b_num = [B(f"num{j}") for j in range(2)]
            b_den = [B(f"den{j}") for j in range(2)]
            b_rden = [B(f"rden{j}") for j in range(2)]
            b_wkv = [B(f"wkv{j}") for j in range(2)]
            b_acar, b_bcar = B("acar"), B("bcar")
            b_psk = [B(f"psk{j}") for j in range(2)]
            b_psv = [B(f"psv{j}") for j in range(2)]
            b_psr = [B(f"psr{j}") for j in range(2)]
            b_psat = [B(f"psat{j}") for j in range(2)]
            b_wload = [B(f"wload{j}") for j in range(2)]
            _bwqa = B("wqa")
            b_wqa = [_bwqa, _bwqa]
            b_wqt = [B(f"wqt{j}") for j in range(2)]
            _bwqsg = B("wqsg")
            b_wqsg = [_bwqsg, _bwqsg]
            _bwqc3 = [B(f"wqc{k}") for k in range(3)]
            b_wqc = [_bwqc3 + [B("wqf0")], _bwqc3 + [B("wqf1")]]
            b_wqbm = [B(f"wqbm{j}") for j in range(2)]
            b_wqrs = [B(f"wqrs{j}") for j in range(2)]
            b_wqo = [B(f"wqo{j}") for j in range(2)]

            for (dst, buf, src) in ((tdc, b_tdc, td2), (ufc, b_ufc, tf2),
                                    (tmkc, b_tmk, tmk2), (tmvc, b_tmv, tmv2),
                                    (tmrc, b_tmr, tmr2), (idn, b_idn, ident_in)):
                S("sp", lambda e, dst=dst, src=src: e.dma_start(out=dst[:, :], in_=src),
                  writes=[buf], dma=True)
            S("act", lambda e: e.activation(ewc[:, :], tdc[:, :], AF.Exp),
              reads=[b_tdc], writes=[b_ewc])
            S("act", lambda e: e.activation(ewc[:, :], ewc[:, :], AF.Exp, scale=-1.0),
              reads=[b_ewc], writes=[b_ewc])
            S("dve", lambda e: e.memset(zrow[:, :], 0.0), writes=[b_zrow])
            b_xz = [B(f"xz{s}") for s in range(SEQ)]
            for s in range(SEQ):
                S("sp", lambda e, s=s: e.dma_start(out=xscr[s, 0:1, :], in_=zrow[:, :]),
                  reads=[b_zrow], writes=[b_xz[s]], dma=True)
            NCH0 = T // TC
            b_xch = [[B(f"xch{s}_{c}") for c in range(NCH0)] for s in range(SEQ)]
            xch_done = set()

            def emit_xcast(s, c):
                if (s, c) in xch_done or c >= NCH0:
                    return
                xch_done.add((s, c))
                t0 = c * TC
                S("pool", lambda e, s=s, t0=t0: e.dma_start(
                    out=xscr[s, 1 + t0:1 + t0 + TC, :],
                    in_=x_in[s * T + t0:s * T + t0 + TC, :]),
                  writes=[b_xch[s][c]], dma=True)

            emit_xcast(0, 0)
            emit_xcast(0, 1)
            for (dst, buf, src) in ((wk_sb, b_wk, wkT), (wv_sb, b_wv, wvT),
                                    (wr_sb, b_wr, wrT), (wo_sb, b_wo, woT)):
                for dc in range(DC):
                    S("pool", lambda e, dst=dst, src=src, dc=dc: e.dma_start(
                        out=dst[:, dc * D:(dc + 1) * D],
                        in_=src[dc * 128:(dc + 1) * 128, :]),
                      writes=[buf], dma=True)

            wq_units = wq_unit_list()
            scr_map = {"g": wgq_scr, "u": wuq_scr, "d": wdq_scr}
            src_map = {"g": wg_in, "u": wu_in, "d": wd_in}

            def emit_quant_front(src_buf, src_ap, w,
                                 a_t, a_b, t_t, t_b, sg_t, sg_b,
                                 bm_t, bm_b, rs_t, rs_b, f_t, f_b):
                nb = w // 16
                aT, tT = a_t[:, 0:w], t_t[:, 0:w]
                fT = f_t[:, 0:w]
                sgT = sg_t[:, 0:w]
                bmT = bm_t[:, 0:nb]
                rsT = rs_t[:, 0:nb]
                ssT = rs_t[:, nb:2 * nb]
                S("act", lambda e: e.activation(aT, src_ap, AF.Abs),
                  reads=[src_buf], writes=[a_b])
                S("act", lambda e: e.activation(sgT, src_ap, AF.Sign),
                  reads=[src_buf], writes=[sg_b])
                S("dve", lambda e: e.reduce_max(
                    bmT, src_ap.rearrange("p (b w) -> p b w", w=16),
                    AX.X, apply_absolute_value=True),
                  reads=[src_buf], writes=[bm_b])
                S("dve", lambda e: e.tensor_scalar(rsT, bmT, 6e-12, None, ALU.max),
                  reads=[bm_b], writes=[rs_b])
                S("dve", lambda e: e.reciprocal(rsT, rsT), reads=[rs_b], writes=[rs_b])
                S("dve", lambda e: e.tensor_scalar(rsT, rsT, 6.0, None, ALU.mult),
                  reads=[rs_b], writes=[rs_b])
                S("dve", lambda e: e.tensor_scalar(ssT, bmT, 1.0 / 6.0, 1e-12,
                                                   ALU.mult, ALU.max),
                  reads=[bm_b], writes=[rs_b])
                rs_bc = rs_t[:, 0:nb].unsqueeze(2).broadcast_to([128, nb, 16])
                s_bc = rs_t[:, nb:2 * nb].unsqueeze(2).broadcast_to([128, nb, 16])
                S("pool", lambda e: e.tensor_tensor(tT, aT, rs_bc, ALU.mult),
                  reads=[a_b, rs_b], writes=[t_b])
                S("pool", lambda e: e.tensor_tensor(fT, sgT, s_bc, ALU.mult),
                  reads=[sg_b, rs_b], writes=[f_b])

            def emit_quant_back(w, t_t, t_b, c_ts, c_bs, outs, out_buf):
                tT = t_t[:, 0:w]
                c0, c1, c2 = (c_ts[0][:, 0:w], c_ts[1][:, 0:w], c_ts[2][:, 0:w])
                fT = c_ts[3][:, 0:w]
                S("dve", lambda e: e.tensor_scalar(c0, tT, QTH[0], QDL[0], ALU.is_gt, ALU.mult),
                  reads=[t_b], writes=[c_bs[0]])
                S("dve", lambda e: e.tensor_scalar(c1, tT, QTH[1], QDL[1], ALU.is_gt, ALU.mult),
                  reads=[t_b], writes=[c_bs[1]])
                S("dve", lambda e: e.tensor_tensor(c0, c0, c1, ALU.add),
                  reads=[c_bs[0], c_bs[1]], writes=[c_bs[0]])
                S("dve", lambda e: e.tensor_scalar(c1, tT, QTH[2], QDL[2], ALU.is_gt, ALU.mult),
                  reads=[t_b], writes=[c_bs[1]])
                S("dve", lambda e: e.tensor_scalar(c2, tT, QTH[3], QDL[3], ALU.is_gt, ALU.mult),
                  reads=[t_b], writes=[c_bs[2]])
                S("dve", lambda e: e.tensor_tensor(c1, c1, c2, ALU.add),
                  reads=[c_bs[1], c_bs[2]], writes=[c_bs[1]])
                S("dve", lambda e: e.tensor_tensor(c0, c0, c1, ALU.add),
                  reads=[c_bs[0], c_bs[1]], writes=[c_bs[0]])
                S("dve", lambda e: e.tensor_scalar(c1, tT, QTH[4], QDL[4], ALU.is_gt, ALU.mult),
                  reads=[t_b], writes=[c_bs[1]])
                S("dve", lambda e: e.tensor_scalar(c2, tT, QTH[5], QDL[5], ALU.is_gt, ALU.mult),
                  reads=[t_b], writes=[c_bs[2]])
                S("dve", lambda e: e.tensor_tensor(c1, c1, c2, ALU.add),
                  reads=[c_bs[1], c_bs[2]], writes=[c_bs[1]])
                S("dve", lambda e: e.tensor_scalar(c2, tT, QTH[6], QDL[6], ALU.is_gt, ALU.mult),
                  reads=[t_b], writes=[c_bs[2]])
                S("dve", lambda e: e.tensor_tensor(c1, c1, c2, ALU.add),
                  reads=[c_bs[1], c_bs[2]], writes=[c_bs[1]])
                S("dve", lambda e: e.tensor_tensor(c0, c0, c1, ALU.add),
                  reads=[c_bs[0], c_bs[1]], writes=[c_bs[0]])
                S("dve", lambda e: e.tensor_tensor(outs, c0, fT, ALU.mult),
                  reads=[c_bs[0], c_bs[3]], writes=[out_buf])

            def emit_quant(src_buf, src_ap, w, outs, out_buf,
                           a_t, a_b, t_t, t_b, c_ts, c_bs, sg_t, sg_b,
                           bm_t, bm_b, rs_t, rs_b):
                emit_quant_front(src_buf, src_ap, w, a_t, a_b, t_t, t_b,
                                 sg_t, sg_b, bm_t, bm_b, rs_t, rs_b,
                                 c_ts[3], c_bs[3])
                emit_quant_back(w, t_t, t_b, c_ts, c_bs, outs, out_buf)

            wq_pending_store = []
            wq_pending_back = []
            wq_loaded = [0]

            def emit_wq_load(j):
                which, row0, col0, w = wq_units[j]
                jb = j % 2
                S("sp", lambda e, which=which, row0=row0, col0=col0, w=w, jb=jb:
                  e.dma_start(out=wload[jb][:, 0:w],
                              in_=src_map[which][row0:row0 + 128, col0:col0 + w]),
                  writes=[b_wload[jb]], dma=True)

            def emit_wq_unit(j):
                while wq_loaded[0] <= min(j + 1, len(wq_units) - 1):
                    emit_wq_load(wq_loaded[0])
                    wq_loaded[0] += 1
                which, row0, col0, w = wq_units[j]
                jb = j % 2
                lw = wload[jb][:, 0:w]
                emit_quant_front(b_wload[jb], lw, w,
                                 wq_a[jb], b_wqa[jb], wq_t[jb], b_wqt[jb],
                                 wq_sg[jb], b_wqsg[jb], wq_bm[jb], b_wqbm[jb],
                                 wq_rs[jb], b_wqrs[jb], wq_c[jb][3], b_wqc[jb][3])
                while wq_pending_back:
                    wq_pending_back.pop(0)()
                while wq_pending_store:
                    wq_pending_store.pop(0)()

                def back(which=which, row0=row0, col0=col0, w=w, jb=jb):
                    emit_quant_back(w, wq_t[jb], b_wqt[jb], wq_c[jb], b_wqc[jb],
                                    wq_o[jb][:, 0:w], b_wqo[jb])
                    wq_pending_store.append(
                        lambda: S("sp", lambda e: e.dma_start(
                            out=scr_map[which][row0:row0 + 128, col0:col0 + w],
                            in_=wq_o[jb][:, 0:w]),
                            reads=[b_wqo[jb]], writes=[b_wqscr[which]], dma=True))
                wq_pending_back.append(back)

            NCH = T // TC
            wq_next = [0]

            def emit_xloads(it):
                if it >= NI:
                    return
                s = it // NCH
                c = it % NCH
                t0 = c * TC
                ib = it % 3
                emit_xcast(s, c)
                xa_rd = [b_xch[s][c]]
                xb_rd = [b_xch[s][c], b_xch[s][c - 1] if c > 0 else b_xz[s]]
                for dc in range(DC):
                    S("sp", lambda e, s=s, t0=t0, dc=dc, ib=ib: e.dma_start(
                        out=xa[ib][:, dc * TC:(dc + 1) * TC],
                        in_=xscr[s, 1 + t0:1 + t0 + TC, dc * 128:(dc + 1) * 128],
                        transpose=True),
                      reads=xa_rd, writes=[b_xa[ib]], dma=True)
                    S("sp", lambda e, s=s, t0=t0, dc=dc, ib=ib: e.dma_start(
                        out=xb[ib][:, dc * TC:(dc + 1) * TC],
                        in_=xscr[s, t0:t0 + TC, dc * 128:(dc + 1) * 128],
                        transpose=True),
                      reads=xb_rd, writes=[b_xb[ib]], dma=True)

            def emit_mixes(it):
                if it >= NI:
                    return
                ib = it % 3
                mb = it % 2
                S("dve", lambda e, ib=ib, mb=mb: e.tensor_tensor(
                    dxb[mb][:, :], xa[ib][:, :], xb[ib][:, :], ALU.subtract),
                  reads=[b_xa[ib], b_xb[ib]], writes=[b_dxb[mb]])
                step = [0]
                for (mx, bmx, tmt, btm) in ((mixk[mb], b_mixk[mb], tmkc, b_tmk),
                                            (mixv[mb], b_mixv[mb], tmvc, b_tmv),
                                            (mixr[mb], b_mixr[mb], tmrc, b_tmr)):
                    for dc in range(DC):
                        sl = slice(dc * TC, (dc + 1) * TC)
                        bm4 = b_mtmp4[step[0] % 4]
                        step[0] += 1
                        S("dve", lambda e, sl=sl, tmt=tmt, mb=mb, dc=dc: e.tensor_scalar(
                            mtmp[mb][:, sl], dxb[mb][:, sl], tmt[:, dc:dc + 1],
                            None, ALU.mult),
                          reads=[b_dxb[mb], btm], writes=[bm4])
                        S("pool", lambda e, sl=sl, mx=mx, ib=ib, mb=mb: e.tensor_tensor(
                            mx[:, sl], mtmp[mb][:, sl], xb[ib][:, sl], ALU.add),
                          reads=[bm4, b_xb[ib]], writes=[bmx])

            def emit_attn(it):
                if it < 0:
                    return
                s = it // NCH
                gt0 = s * T + (it % NCH) * TC
                ib = it % 3
                rb = it % 2
                zb = it % 2
                for fc in range(DC):
                    pb = fc % 2
                    for ec in range(DC):
                        S("pe", lambda e, fc=fc, ec=ec, pb=pb, rb=rb: e.matmul(
                            ps_at[pb][:, :],
                            wo_sb[:, ec * D + fc * 128: ec * D + (fc + 1) * 128],
                            rwkv_t[rb][:, ec * TC:(ec + 1) * TC],
                            start=(ec == 0), stop=(ec == DC - 1)),
                          reads=[b_wo, b_rwkv[rb]], writes=[b_psat[pb]])
                    sl = slice(fc * TC, (fc + 1) * TC)
                    S("dve", lambda e, sl=sl, pb=pb, ib=ib, zb=zb: e.tensor_tensor(
                        z16_t[zb][:, sl], xa[ib][:, sl], ps_at[pb][:, :], ALU.add),
                      reads=[b_xa[ib], b_psat[pb]], writes=[b_z16[zb]])
                S("sp", lambda e, it=it, zb=zb: e.dma_start(
                    out=zscr[it, :, :], in_=z16_t[zb][:, :]),
                  reads=[b_z16[zb]], writes=[b_zscr], dma=True)

            while wq_loaded[0] < 2:
                emit_wq_load(wq_loaded[0])
                wq_loaded[0] += 1
            emit_xloads(0)
            emit_xloads(1)
            emit_mixes(0)
            wkv_pend = []
            for it in range(NI):
                c = it % NCH
                mb = it % 2
                while wkv_pend:
                    wkv_pend.pop(0)()
                while wq_loaded[0] <= min(wq_next[0] + 1, len(wq_units) - 1):
                    emit_wq_load(wq_loaded[0])
                    wq_loaded[0] += 1
                for (mix, bmix, wsb, bwsb, kind) in (
                        (mixk[mb], b_mixk[mb], wk_sb, b_wk, "k"),
                        (mixv[mb], b_mixv[mb], wv_sb, b_wv, "v"),
                        (mixr[mb], b_mixr[mb], wr_sb, b_wr, "r")):
                    for ec in range(DC):
                        pb = ec % 2
                        pss, bpss = {"k": (ps_k, b_psk), "v": (ps_v, b_psv),
                                     "r": (ps_r, b_psr)}[kind]
                        for dc in range(DC):
                            S("pe", lambda e, ec=ec, dc=dc, wsb=wsb, mix=mix, pb=pb,
                              pss=pss: e.matmul(
                                pss[pb][:, :],
                                wsb[:, dc * D + ec * 128: dc * D + (ec + 1) * 128],
                                mix[:, dc * TC:(dc + 1) * TC],
                                start=(dc == 0), stop=(dc == DC - 1)),
                              reads=[bwsb, bmix], writes=[bpss[pb]])
                        sl = slice(ec * TC, (ec + 1) * TC)
                        if kind == "k":
                            S("act", lambda e, sl=sl, pb=pb: e.activation(
                                ek_t[:, sl], ps_k[pb][:, :], AF.Exp),
                              reads=[b_psk[pb]], writes=[b_ek])
                            S("act", lambda e, sl=sl, pb=pb, ec=ec: e.activation(
                                euk_t[:, sl], ps_k[pb][:, :], AF.Exp,
                                bias=ufc[:, ec:ec + 1]),
                              reads=[b_psk[pb], b_ufc], writes=[b_euk])
                        elif kind == "v":
                            S("act", lambda e, sl=sl, pb=pb: e.activation(
                                vv_t[:, sl], ps_v[pb][:, :], AF.Copy),
                              reads=[b_psv[pb]], writes=[b_vv])
                        else:
                            eb = ec % 2
                            S("act", lambda e, pb=pb, eb=eb: e.activation(
                                er_t[eb][:, :], ps_r[pb][:, :], AF.Exp, scale=-1.0),
                              reads=[b_psr[pb]], writes=[b_er[eb]])
                            S("act", lambda e, eb=eb: e.activation(
                                l1_t[eb][:, :], er_t[eb][:, :], AF.Ln, bias=1.0),
                              reads=[b_er[eb]], writes=[b_l1[eb]])
                            S("act", lambda e, sl=sl, eb=eb: e.activation(
                                sr_t[:, sl], l1_t[eb][:, :], AF.Exp, scale=-1.0),
                              reads=[b_l1[eb]], writes=[b_sr])
                emit_mixes(it + 1)
                share = ceil_div(len(wq_units) - wq_next[0], NI - it)
                done = 0
                for _ in range(min(2, share)):
                    if wq_next[0] < len(wq_units):
                        emit_wq_unit(wq_next[0])
                        wq_next[0] += 1
                        done += 1
                emit_attn(it - 1)
                for _ in range(share - done):
                    if wq_next[0] < len(wq_units):
                        emit_wq_unit(wq_next[0])
                        wq_next[0] += 1
                emit_xloads(it + 2)
                S("dve", lambda e: e.tensor_tensor(ekv_t[:, :], ek_t[:, :], vv_t[:, :],
                                                   ALU.mult),
                  reads=[b_ek, b_vv], writes=[b_ekv])
                S("dve", lambda e: e.tensor_tensor(eukv_t[:, :], euk_t[:, :], vv_t[:, :],
                                                   ALU.mult),
                  reads=[b_euk, b_vv], writes=[b_eukv])
                rb = it % 2
                for dc in range(DC):
                    ab = dc % 2
                    sl = slice(dc * TC, (dc + 1) * TC)
                    if c == 0:
                        S("dve", lambda e, dc=dc: e.memset(acar[:, dc:dc + 1], 0.0),
                          writes=[b_acar])
                        S("dve", lambda e, dc=dc: e.memset(bcar[:, dc:dc + 1], 0.0),
                          writes=[b_bcar])
                    S("dve", lambda e, dc=dc, ab=ab: e.tensor_copy(
                        alf[ab][:, 0:1], acar[:, dc:dc + 1]),
                      reads=[b_acar], writes=[b_alf[ab]])
                    S("dve", lambda e, dc=dc, ab=ab, sl=sl: e.tensor_tensor_scan(
                        alf[ab][:, 1:TC + 1], ewc[:, dc:dc + 1].broadcast_to([128, TC]),
                        ekv_t[:, sl], acar[:, dc:dc + 1], ALU.mult, ALU.add),
                      reads=[b_ewc, b_ekv, b_acar], writes=[b_alf[ab]])
                    S("dve", lambda e, dc=dc, ab=ab: e.tensor_copy(
                        acar[:, dc:dc + 1], alf[ab][:, TC:TC + 1]),
                      reads=[b_alf[ab]], writes=[b_acar])
                    S("dve", lambda e, dc=dc, ab=ab: e.tensor_copy(
                        bet[ab][:, 0:1], bcar[:, dc:dc + 1]),
                      reads=[b_bcar], writes=[b_bet[ab]])
                    S("dve", lambda e, dc=dc, ab=ab, sl=sl: e.tensor_tensor_scan(
                        bet[ab][:, 1:TC + 1], ewc[:, dc:dc + 1].broadcast_to([128, TC]),
                        ek_t[:, sl], bcar[:, dc:dc + 1], ALU.mult, ALU.add),
                      reads=[b_ewc, b_ek, b_bcar], writes=[b_bet[ab]])
                    S("dve", lambda e, dc=dc, ab=ab: e.tensor_copy(
                        bcar[:, dc:dc + 1], bet[ab][:, TC:TC + 1]),
                      reads=[b_bet[ab]], writes=[b_bcar])
                    if ab == 1:
                        if len(wkv_pend) > 1:
                            wkv_pend.pop(0)()
                        pr = (dc // 2) % 2
                        sl2 = slice((dc - 1) * TC, (dc + 1) * TC)
                        a_sh = alfp.rearrange("p (s c) -> p s c", s=2)[:, :, 0:TC]
                        b_sh = betp.rearrange("p (s c) -> p s c", s=2)[:, :, 0:TC]
                        S("dve", lambda e, pr=pr, sl2=sl2, a_sh=a_sh: e.tensor_tensor(
                            numt[pr][:, :], a_sh, eukv_t[:, sl2], ALU.add),
                          reads=[b_alf[0], b_alf[1], b_eukv], writes=[b_num[pr]])
                        S("dve", lambda e, pr=pr, sl2=sl2, b_sh=b_sh: e.tensor_tensor(
                            dent[pr][:, :], b_sh, euk_t[:, sl2], ALU.add),
                          reads=[b_bet[0], b_bet[1], b_euk], writes=[b_den[pr]])
                        S("act", lambda e, pr=pr: e.activation(
                            dent[pr][:, :], dent[pr][:, :], AF.Ln),
                          reads=[b_den[pr]], writes=[b_den[pr]])
                        S("act", lambda e, pr=pr: e.activation(
                            rden[pr][:, :], dent[pr][:, :], AF.Exp, scale=-1.0),
                          reads=[b_den[pr]], writes=[b_rden[pr]])
                        def wkv_fin(pr=pr, sl2=sl2, rb=rb):
                            S("dve", lambda e: e.tensor_tensor(
                                wkvt[pr][:, :], numt[pr][:, :], rden[pr][:, :],
                                ALU.mult),
                              reads=[b_num[pr], b_rden[pr]], writes=[b_wkv[pr]])
                            S("dve", lambda e: e.tensor_tensor(
                                rwkv_t[rb][:, sl2], wkvt[pr][:, :], sr_t[:, sl2],
                                ALU.mult),
                              reads=[b_wkv[pr], b_sr], writes=[b_rwkv[rb]])
                        wkv_pend.append(wkv_fin)
            while wkv_pend:
                wkv_pend.pop(0)()
            emit_attn(NI - 1)
            while wq_next[0] < len(wq_units):
                emit_wq_unit(wq_next[0])
                wq_next[0] += 1
            while wq_pending_back:
                wq_pending_back.pop(0)()
            while wq_pending_store:
                wq_pending_store.pop(0)()

            phase1_end = {e: len(sch.items[e]) for e in ENGS}

            for sname in sch.sem_names:
                if sname not in sems:
                    sems[sname] = sem_es.enter_context(nc.semaphore(sname))

            with nc.Block() as blk1:
                def mk_body(ename, lo, hi):
                    def body(eng):
                        seen = {}
                        for fn, waits, inc in sch.items[ename][lo:hi]:
                            for we, wv in sorted(waits.items()):
                                if seen.get(we, -1) < wv:
                                    eng.wait_ge(sems[we], wv)
                                    seen[we] = wv
                            fn(eng).then_inc(sems[inc[0]], inc[1])
                    return body
                blk1.sync(mk_body("sp", 0, phase1_end["sp"]))
                blk1.tensor(mk_body("pe", 0, phase1_end["pe"]))
                blk1.scalar(mk_body("act", 0, phase1_end["act"]))
                blk1.vector(mk_body("dve", 0, phase1_end["dve"]))
                blk1.gpsimd(mk_body("pool", 0, phase1_end["pool"]))
            es1.close()

            sch.fence_all()
            es2 = ExitStack()
            sbt = lambda n, s, d: es2.enter_context(nc.sbuf_tensor(n, s, d))
            pst = lambda n, s, d: es2.enter_context(nc.psum_tensor(n, s, d))

            wgqT = sbt("wgqT", [128, DC * I], BF16)
            wuqT = sbt("wuqT", [128, DC * I], BF16)
            wdqT = sbt("wdqT", [128, (I // 128) * D], BF16)
            z16t = [sbt(f"z16t{j}", [128, D], BF16) for j in range(2)]
            HW2 = 1024
            _ha = sbt("h_a", [128, HW2], BF16)
            h_a = [_ha, _ha]
            _htq = sbt("h_tq", [128, HW2], BF16)
            h_tq = [_htq, _htq]
            _hc = [sbt(f"h_c{k}", [128, HW2], BF16) for k in range(3)]
            h_f = [sbt(f"h_f{j}", [128, HW2], BF16) for j in range(2)]
            h_c = [_hc + [h_f[0]], _hc + [h_f[1]]]
            _hsg = sbt("h_sg", [128, HW2], BF16)
            h_sg = [_hsg, _hsg]
            h_bm = [sbt(f"h_bm{j}", [128, HW2 // 16], F32) for j in range(2)]
            h_rs = [sbt(f"h_rs{j}", [128, 2 * (HW2 // 16)], F32) for j in range(2)]
            yq = sbt("yq", [128, D], BF16)
            yqT = [sbt(f"yqT{j}", [128, D], BF16) for j in range(2)]
            em_t = [sbt(f"em_t{j}", [128, 512], BF16) for j in range(2)]
            _el = sbt("el_t", [128, 512], F32)
            el_t = [_el, _el]
            h_t = [sbt(f"h_t{j}", [128, HW2], BF16) for j in range(2)]
            hq = [sbt(f"hq{j}", [128, HW2 if j < 2 else 768], BF16)
                  for j in range(3)]
            hqT = [sbt(f"hqT{j}", [128, HW2 if j < 2 else 768], BF16)
                   for j in range(3)]
            z2 = sbt("z2", [128, D], F32)
            y32 = [sbt(f"y32_{j}", [128, D], F32) for j in range(3)]
            o32 = [sbt(f"o32_{j}", [128, D], F32) for j in range(2)]
            st1 = [sbt(f"st1_{j}", [128, 16], F32) for j in range(2)]
            qt2 = sbt("q_t2", [128, HW2], BF16)

            qts = [h_tq[0], qt2]

            ps_tr = [pst(f"ps_tr{j}", [128, D], BF16) for j in range(2)]
            ps_g = [pst(f"ps_g{j}", [128, 512], F32) for j in range(2)]
            ps_u = [pst(f"ps_u{j}", [128, 512], F32) for j in range(2)]
            ps_dn = pst("ps_dn", [128, D], F32)

            b_wgqT, b_wuqT, b_wdqT = B("wgqT"), B("wuqT"), B("wdqT")
            b_z16t = [B(f"z16t{j}") for j in range(2)]
            _bha = B("ha")
            b_ha = [_bha, _bha]
            _bhtq = B("htq")
            b_htq = [_bhtq, _bhtq]
            _bhc = [B(f"hc{k}") for k in range(3)]
            b_hc = [_bhc + [B("hf0")], _bhc + [B("hf1")]]
            _bhsg = B("hsg")
            b_hsg = [_bhsg, _bhsg]
            b_hbm = [B(f"hbm{j}") for j in range(2)]
            b_hrs = [B(f"hrs{j}") for j in range(2)]
            b_yq = B("yq")
            b_yqT = [B(f"yqT{j}") for j in range(2)]
            b_em = [B(f"em{j}") for j in range(2)]
            _bel = B("el")
            b_el = [_bel, _bel]
            b_h = [B(f"h{j}") for j in range(2)]
            b_hq = [B(f"hq{j}") for j in range(3)]
            b_hqT = [B(f"hqT{j}") for j in range(3)]
            b_z2 = B("z2")
            b_y32 = [B(f"y32{j}") for j in range(3)]
            b_o32 = [B(f"o32{j}") for j in range(2)]
            b_st1L = [B(f"st1L{j}") for j in range(2)]
            b_st1H = [B(f"st1H{j}") for j in range(2)]
            b_qts = [b_htq[0], B("qt2")]
            b_pstr = [B("pstr0"), B("pstr1")]
            b_psg = [B(f"psg{j}") for j in range(2)]
            b_psu = [B(f"psu{j}") for j in range(2)]
            b_psdn = B("psdn")

            def emit_zload(ti):
                if ti >= NP2:
                    return
                g0 = ti * 128
                it_src = g0 // TC
                toff = g0 % TC
                zb = ti % 2
                for dc in range(DC):
                    S("sp", lambda e, dc=dc, it_src=it_src, toff=toff, zb=zb:
                      e.dma_start(
                        out=z16t[zb][:, dc * 128:(dc + 1) * 128],
                        in_=zscr[it_src, 0:128, dc * TC + toff:dc * TC + toff + 128],
                        transpose=True),
                      reads=[b_zscr], writes=[b_z16t[zb]], dma=True)

            emit_zload(0)
            emit_zload(1)
            for dc in range(DC):
                S("sp", lambda e, dc=dc: e.dma_start(
                    out=wgqT[:, dc * I:(dc + 1) * I],
                    in_=wgq_scr[:, dc * 128:(dc + 1) * 128], transpose=True),
                  reads=[b_wqscr["g"]], writes=[b_wgqT], dma=True)
            for dc in range(DC):
                S("sp", lambda e, dc=dc: e.dma_start(
                    out=wuqT[:, dc * I:(dc + 1) * I],
                    in_=wuq_scr[:, dc * 128:(dc + 1) * 128], transpose=True),
                  reads=[b_wqscr["u"]], writes=[b_wuqT], dma=True)
            for ic in range(I // 128):
                S("sp", lambda e, ic=ic: e.dma_start(
                    out=wdqT[:, ic * D:(ic + 1) * D],
                    in_=wdq_scr[:, ic * 128:(ic + 1) * 128], transpose=True),
                  reads=[b_wqscr["d"]], writes=[b_wdqT], dma=True)

            for j in range(2):
                S("dve", lambda e, j=j: e.memset(st1[j][:, 7:8], LN_EPS),
                  writes=[b_st1L[j]])
                S("dve", lambda e, j=j: e.memset(st1[j][:, 15:16], LN_EPS),
                  writes=[b_st1H[j]])

            def emit_ln(sb, base, bs1, z_ap, z_buf, o_ap, o_buf):
                s1 = st1[sb]
                b0 = base
                S("dve", lambda e: e.tensor_scalar(s1[:, b0 + 2:b0 + 4],
                                                   s1[:, b0:b0 + 2], 1.0 / D,
                                                   None, ALU.mult),
                  reads=[bs1], writes=[bs1])
                S("dve", lambda e: e.tensor_scalar(s1[:, b0 + 4:b0 + 5],
                                                   s1[:, b0 + 2:b0 + 3],
                                                   s1[:, b0 + 2:b0 + 3], None, ALU.mult),
                  reads=[bs1], writes=[bs1])
                S("dve", lambda e: e.tensor_tensor(s1[:, b0 + 4:b0 + 5],
                                                   s1[:, b0 + 3:b0 + 4],
                                                   s1[:, b0 + 4:b0 + 5], ALU.subtract),
                  reads=[bs1], writes=[bs1])
                S("act", lambda e: e.activation(s1[:, b0 + 5:b0 + 6],
                                                s1[:, b0 + 4:b0 + 5], AF.Ln,
                                                bias=s1[:, b0 + 7:b0 + 8]),
                  reads=[bs1], writes=[bs1])
                S("act", lambda e: e.activation(s1[:, b0 + 6:b0 + 7],
                                                s1[:, b0 + 5:b0 + 6], AF.Exp,
                                                scale=-0.5),
                  reads=[bs1], writes=[bs1])
                S("dve", lambda e: e.tensor_scalar(o_ap, z_ap, s1[:, b0 + 2:b0 + 3],
                                                   s1[:, b0 + 6:b0 + 7],
                                                   ALU.subtract, ALU.mult),
                  reads=[z_buf, bs1], writes=[o_buf])

            qsk = [0]
            qc_t = _hc
            qc_b = _bhc

            def qfront(src_buf, src_ap, w):
                par = qsk[0] % 2
                qsk[0] += 1
                emit_quant_front(src_buf, src_ap, w,
                                 h_a[0], b_ha[0], qts[par], b_qts[par],
                                 h_sg[0], b_hsg[0], h_bm[par], b_hbm[par],
                                 h_rs[par], b_hrs[par], h_f[par], b_hf(par))
                return par

            def b_hf(par):
                return b_hc[par][3]

            def qback(par, w, outs, out_buf):
                emit_quant_back(w, qts[par], b_qts[par],
                                qc_t + [h_f[par]], qc_b + [b_hf(par)],
                                outs, out_buf)

            def emit_stats(ti):
                p = ti % 2
                S("act", lambda e, p=p: e.activation(
                    z2[:, :], z16t[p][:, :], AF.Copy, accum_out=st1[p][:, 0:1]),
                  reads=[b_z16t[p]], writes=[b_z2, b_st1L[p]])
                S("act", lambda e, p=p: e.activation(
                    z2[:, :], z16t[p][:, :], AF.Square, accum_out=st1[p][:, 1:2]),
                  reads=[b_z16t[p]], writes=[b_z2, b_st1L[p]])

            def emit_yunit(ti):
                p = ti % 2
                p3 = ti % 3
                emit_ln(p, 0, b_st1L[p], z16t[p][:, :], b_z16t[p],
                        y32[p3][:, :], b_y32[p3])
                return qfront(b_y32[p3], y32[p3][:, :], D)

            def emit_yback(ti, par):
                qback(par, D, yq[:, :], b_yq)

            def emit_ytrans(ti):
                p = ti % 2
                for dc in range(DC):
                    S("pe", lambda e, dc=dc: e.transpose(
                        ps_tr[1][:, dc * 128:(dc + 1) * 128],
                        yq[:, dc * 128:(dc + 1) * 128], idn[:, :]),
                      reads=[b_yq, b_idn], writes=[b_pstr[1]])
                S("act", lambda e, p=p: e.activation(yqT[p][:, :], ps_tr[1][:, :],
                                                     AF.Copy),
                  reads=[b_pstr[1]], writes=[b_yqT[p]])

            def emit_piece_tail(p0, pw, dpi):
                nic = pw // 128
                pb = dpi % 2
                for j in range(nic):
                    S("pe", lambda e, j=j, dpi=dpi, pb=pb: e.transpose(
                        ps_tr[pb][:, j * 128:(j + 1) * 128],
                        hq[dpi][:, j * 128:(j + 1) * 128], idn[:, :]),
                      reads=[b_hq[dpi], b_idn], writes=[b_pstr[pb]])
                S("act", lambda e, nic=nic, dpi=dpi, pb=pb: e.activation(
                    hqT[dpi][:, 0:nic * 128], ps_tr[pb][:, 0:nic * 128], AF.Copy),
                  reads=[b_pstr[pb]], writes=[b_hqT[dpi]])
                for j in range(nic):
                    ic = (p0 // 128) + j
                    for nh in range(2):
                        S("pe", lambda e, j=j, ic=ic, nh=nh, dpi=dpi,
                          first=(p0 == 0 and j == 0): e.matmul(
                            ps_dn[:, nh * 512:(nh + 1) * 512],
                            hqT[dpi][:, j * 128:(j + 1) * 128],
                            wdqT[:, ic * D + nh * 512: ic * D + (nh + 1) * 512],
                            start=first, stop=(ic == I // 128 - 1)),
                          reads=[b_hqT[dpi], b_wdqT], writes=[b_psdn])

            pending_tails = []
            pt_ctr = [0]

            def emit_dpback(info):
                d0, dw, dpi, par, owner = info
                qback(par, dw, hq[dpi][:, 0:dw], b_hq[dpi])
                zd = owner if (d0 + dw == I) else None
                pending_tails.append(
                    (pt_ctr[0], lambda: emit_piece_tail(d0, dw, dpi), zd))

            def flush_tails_aged(final=False):
                zd = None
                while pending_tails and (final or
                                         pending_tails[0][0] <= pt_ctr[0] - 2):
                    _, fn, z = pending_tails.pop(0)
                    fn()
                    if z is not None:
                        zd = z
                pt_ctr[0] += 1
                return zd

            def emit_zunit(ti):
                p = ti % 2
                p3 = ti % 3
                gt0 = ti * 128
                S("dve", lambda e, p=p, p3=p3: e.scalar_tensor_tensor(
                    z2[:, :], ps_dn[:, :], 1.0, y32[p3][:, :], ALU.mult, ALU.add,
                    accum_out=st1[p][:, 8:9]),
                  reads=[b_psdn, b_y32[p3]], writes=[b_z2, b_st1H[p]])
                S("act", lambda e, p=p: e.activation(h_a[0][:, :], z2[:, :], AF.Square,
                                                     accum_out=st1[p][:, 9:10]),
                  reads=[b_z2], writes=[b_ha[0], b_st1H[p]])
                emit_ln(p, 8, b_st1H[p], z2[:, :], b_z2, o32[p][:, :], b_o32[p])
                S("act", lambda e, gt0=gt0, p=p: e.dma_start(
                    out=out[gt0:gt0 + 128, :], in_=o32[p][:, :]),
                  reads=[b_o32[p]], writes=[b_out], dma=True)

            dpcs = [(0, 1024), (1024, 1024), (2048, 768)]

            emit_stats(0)
            ypar = emit_yunit(0)
            emit_yback(0, ypar)
            emit_ytrans(0)

            dp2_pend = [None]
            for ti in range(NP2):
                yti = ti + 1
                zti = ti - 1
                if yti < NP2:
                    emit_stats(yti)
                emit_zload(ti + 2)
                if dp2_pend[0] is not None:
                    emit_dpback(dp2_pend.pop(0))
                    dp2_pend.append(None)
                if yti < NP2:
                    ypar = emit_yunit(yti)

                pending_dp = []
                for dpi, (d0, dw) in enumerate(dpcs):
                    dpb = dpi % 2
                    nhalf = ceil_div(dw, 512)
                    halves = []
                    for hf in range(nhalf):
                        p0 = d0 + hf * 512
                        pw = min(512, d0 + dw - p0)
                        hb = hf % 2
                        halves.append((hf, p0, pw, hb))
                        for dc in range(DC):
                            S("pe", lambda e, dc=dc, p0=p0, pw=pw, hb=hb, ti=ti:
                              e.matmul(
                                ps_u[hb][:, 0:pw],
                                yqT[ti % 2][:, dc * 128:(dc + 1) * 128],
                                wuqT[:, dc * I + p0: dc * I + p0 + pw],
                                start=(dc == 0), stop=(dc == DC - 1)),
                              reads=[b_yqT[ti % 2], b_wuqT], writes=[b_psu[hb]])
                        for dc in range(DC):
                            S("pe", lambda e, dc=dc, p0=p0, pw=pw, hb=hb, ti=ti:
                              e.matmul(
                                ps_g[hb][:, 0:pw],
                                yqT[ti % 2][:, dc * 128:(dc + 1) * 128],
                                wgqT[:, dc * I + p0: dc * I + p0 + pw],
                                start=(dc == 0), stop=(dc == DC - 1)),
                              reads=[b_yqT[ti % 2], b_wgqT], writes=[b_psg[hb]])
                    for hf, p0, pw, hb in halves:
                        S("act", lambda e, pw=pw, hb=hb: e.activation(
                            em_t[hb][:, 0:pw], ps_g[hb][:, 0:pw], AF.Exp, scale=-1.0),
                          reads=[b_psg[hb]], writes=[b_em[hb]])
                        S("act", lambda e, pw=pw, hb=hb: e.activation(
                            el_t[hb][:, 0:pw], em_t[hb][:, 0:pw], AF.Ln, bias=1.0),
                          reads=[b_em[hb]], writes=[b_el[hb]])
                        S("act", lambda e, pw=pw, hb=hb: e.activation(
                            em_t[hb][:, 0:pw], el_t[hb][:, 0:pw], AF.Exp, scale=-1.0),
                          reads=[b_el[hb]], writes=[b_em[hb]])
                        S("dve", lambda e, pw=pw, hb=hb: e.tensor_tensor(
                            em_t[hb][:, 0:pw], ps_u[hb][:, 0:pw], em_t[hb][:, 0:pw],
                            ALU.mult),
                          reads=[b_psu[hb], b_em[hb]], writes=[b_em[hb]])
                        S("dve", lambda e, pw=pw, hb=hb, hf=hf, dpb=dpb: e.tensor_tensor(
                            h_t[dpb][:, hf * 512: hf * 512 + pw],
                            em_t[hb][:, 0:pw], ps_g[hb][:, 0:pw], ALU.mult),
                          reads=[b_em[hb], b_psg[hb]], writes=[b_h[dpb]])
                    zd = flush_tails_aged()
                    if zd is not None:
                        emit_zunit(zd)
                    par = qfront(b_h[dpb], h_t[dpb][:, 0:dw], dw)
                    if pending_dp:
                        emit_dpback(pending_dp.pop(0))
                    elif yti < NP2:
                        emit_yback(yti, ypar)
                    pending_dp.append((d0, dw, dpi, par, ti))
                    if dpi == 1 and yti < NP2:
                        emit_ytrans(yti)
                dp2_pend[0] = pending_dp.pop(0)

            emit_dpback(dp2_pend.pop(0))
            zd = flush_tails_aged(final=True)
            if zd is not None:
                emit_zunit(zd)

            sch.step("sp", lambda e: e.nop(),
                     extra_deps=[("dma_out", 16 * b_out.dma_cnt["dma_out"])])

            for sname in sch.sem_names:
                if sname not in sems:
                    sems[sname] = sem_es.enter_context(nc.semaphore(sname))

            with nc.Block() as blk2:
                def mk_body2(ename):
                    lo = phase1_end[ename]

                    def body(eng):
                        seen = {}
                        for fn, waits, inc in sch.items[ename][lo:]:
                            for we, wv in sorted(waits.items()):
                                if seen.get(we, -1) < wv:
                                    eng.wait_ge(sems[we], wv)
                                    seen[we] = wv
                            fn(eng).then_inc(sems[inc[0]], inc[1])
                    return body
                blk2.sync(mk_body2("sp"))
                blk2.tensor(mk_body2("pe"))
                blk2.scalar(mk_body2("act"))
                blk2.vector(mk_body2("dve"))
                blk2.gpsimd(mk_body2("pool"))
            es2.close()
    return nc


def _prep_inputs(inputs):
    import ml_dtypes
    d = {}
    d["wkT"] = np.ascontiguousarray(inputs["L0_attn_Wk"].T)
    d["wvT"] = np.ascontiguousarray(inputs["L0_attn_Wv"].T)
    d["wrT"] = np.ascontiguousarray(inputs["L0_attn_Wr"].T)
    d["woT"] = np.ascontiguousarray(inputs["L0_attn_Wo"].T)
    d["wg"] = np.ascontiguousarray(inputs["L0_ffn_w_gate"], dtype=np.float32)
    d["wu"] = np.ascontiguousarray(inputs["L0_ffn_w_up"], dtype=np.float32)
    d["wd"] = np.ascontiguousarray(inputs["L0_ffn_w_down"], dtype=np.float32)
    d["td2"] = np.ascontiguousarray(
        np.asarray(inputs["L0_attn_time_decay"], dtype=np.float32).reshape(DC, 128).T)
    d["tf2"] = np.ascontiguousarray(
        np.asarray(inputs["L0_attn_time_first"], dtype=np.float32).reshape(DC, 128).T)
    d["tmk2"] = np.ascontiguousarray(
        np.asarray(inputs["L0_attn_time_mix_k"], dtype=np.float32).reshape(DC, 128).T)
    d["tmv2"] = np.ascontiguousarray(
        np.asarray(inputs["L0_attn_time_mix_v"], dtype=np.float32).reshape(DC, 128).T)
    d["tmr2"] = np.ascontiguousarray(
        np.asarray(inputs["L0_attn_time_mix_r"], dtype=np.float32).reshape(DC, 128).T)
    d["ident"] = np.eye(128, dtype=ml_dtypes.bfloat16)
    return d


def kernel(**inputs):
    inputs = {k: np.asarray(v) for k, v in inputs.items()}
    x = inputs["L0_attn_x"]
    Bfull, T, Dm = x.shape
    assert Dm == D and Bfull == SEQ * 8
    shared = _prep_inputs(inputs)
    nc = build(T=T, TC=256)
    in_maps = []
    for c in range(8):
        m = dict(shared)
        m["x"] = np.ascontiguousarray(
            x[c * SEQ:(c + 1) * SEQ].reshape(SEQ * T, D).astype(np.float32))
        in_maps.append(m)
    res = run_bass_kernel_spmd(nc, in_maps, core_ids=list(range(8)))
    outp = np.empty((Bfull, T, D), dtype=np.float32)
    for c in range(8):
        outp[c * SEQ:(c + 1) * SEQ] = np.asarray(res.results[c]["out"]).reshape(SEQ, T, D)
    return outp


# version 31
# speedup vs baseline: 1.0095x; 1.0095x over previous
import numpy as np
from contextlib import ExitStack

import concourse.bass as bass
import concourse.mybir as mybir
from concourse.bass_utils import run_bass_kernel_spmd

F32 = mybir.dt.float32
BF16 = mybir.dt.bfloat16
AF = mybir.ActivationFunctionType
ALU = mybir.AluOpType
AX = mybir.AxisListType

ENGS = ("sp", "pe", "act", "dve", "pool")

D = 1024
DC = 8
I = 2816
SEQ = 2
LN_EPS = 1e-5

QTH = (0.25, 0.75, 1.25, 1.75, 2.5, 3.5, 5.0)
QDL = (0.5, 0.5, 0.5, 0.5, 1.0, 1.0, 2.0)

WQW_G = 704
WQ_SLOTS = 15


def wq_unit_list():
    units = []
    for ic in range(I // 128):
        units.append(("g", ic * 128, 0, D // 2))
        units.append(("g", ic * 128, D // 2, D // 2))
    for ic in range(I // 128):
        units.append(("u", ic * 128, 0, D // 2))
        units.append(("u", ic * 128, D // 2, D // 2))
    for dc in range(DC):
        for q4 in range(4):
            units.append(("d", dc * 128, q4 * (I // 4), I // 4))
    return units


class Buf:
    __slots__ = ("name", "last_w", "readers", "dma_cnt")

    def __init__(self, name):
        self.name = name
        self.last_w = None
        self.readers = {}
        self.dma_cnt = {}


class Sched:

    def __init__(self):
        self.items = {e: [] for e in ENGS}
        self.cnt = {e: 0 for e in ENGS}
        self.pending_fence = {}
        self.sem_names = list(ENGS)
        self.dmatot = {}

    def fence_all(self):
        snap = [(e, self.cnt[e]) for e in ENGS] + list(self.dmatot.items())
        for e in ENGS:
            self.pending_fence[e] = list(snap)

    def step(self, eng, fn, reads=(), writes=(), dma=False, extra_deps=()):
        deps = list(extra_deps)
        if eng in self.pending_fence:
            deps.extend(self.pending_fence.pop(eng))
        for b in reads:
            deps.append(b.last_w)
        for b in writes:
            deps.append(b.last_w)
            deps.extend(b.readers.items())
        waits = {}
        for d in deps:
            if d is None:
                continue
            de, dv = d
            if eng == "pe" and de == "pe":
                continue
            if dv > 0 and waits.get(de, -1) < dv:
                waits[de] = dv
        if dma:
            assert len(writes) == 1, "DMA must write exactly one Buf"
            wb = writes[0]
            sem = ("dmasw_" if eng == "pool" else "dma_") + wb.name
            if sem not in self.sem_names:
                self.sem_names.append(sem)
            wb.dma_cnt[sem] = wb.dma_cnt.get(sem, 0) + 1
            self.items[eng].append((fn, waits, (sem, 16)))
            tok = (sem, 16 * wb.dma_cnt[sem])
            self.dmatot[sem] = tok[1]
        else:
            self.cnt[eng] += 1
            self.items[eng].append((fn, waits, (eng, 1)))
            tok = (eng, self.cnt[eng])
        for b in reads:
            cur = b.readers.get(tok[0], -1)
            if cur < tok[1]:
                b.readers[tok[0]] = tok[1]
        for b in writes:
            b.last_w = tok
            b.readers = {}
        return tok


def ceil_div(a, b):
    return (a + b - 1) // b


def build(T=2048, TC=256):
    NT = SEQ * T
    NI = SEQ * (T // TC)
    NP2 = NT // 128
    ipcs = []
    p = 0
    while p < I:
        w = min(512, I - p)
        ipcs.append((p, w))
        p += w

    nc = bass.Bass("TRN2", target_bir_lowering=False)

    x_in = nc.dram_tensor("x", [NT, D], F32, kind="ExternalInput").ap()
    wkT = nc.dram_tensor("wkT", [D, D], F32, kind="ExternalInput").ap()
    wvT = nc.dram_tensor("wvT", [D, D], F32, kind="ExternalInput").ap()
    wrT = nc.dram_tensor("wrT", [D, D], F32, kind="ExternalInput").ap()
    woT = nc.dram_tensor("woT", [D, D], F32, kind="ExternalInput").ap()
    wg_in = nc.dram_tensor("wg", [I, D], F32, kind="ExternalInput").ap()
    wu_in = nc.dram_tensor("wu", [I, D], F32, kind="ExternalInput").ap()
    wd_in = nc.dram_tensor("wd", [D, I], F32, kind="ExternalInput").ap()
    td2 = nc.dram_tensor("td2", [128, DC], F32, kind="ExternalInput").ap()
    tf2 = nc.dram_tensor("tf2", [128, DC], F32, kind="ExternalInput").ap()
    tmk2 = nc.dram_tensor("tmk2", [128, DC], F32, kind="ExternalInput").ap()
    tmv2 = nc.dram_tensor("tmv2", [128, DC], F32, kind="ExternalInput").ap()
    tmr2 = nc.dram_tensor("tmr2", [128, DC], F32, kind="ExternalInput").ap()
    ident_in = nc.dram_tensor("ident", [128, 128], BF16, kind="ExternalInput").ap()
    out = nc.dram_tensor("out", [NT, D], F32, kind="ExternalOutput").ap()

    xscr = nc.dram_tensor("xscr", [SEQ, T + 1, D], BF16, kind="Internal").ap()
    zscr = nc.dram_tensor("zscr", [NI, 128, DC * TC], BF16, kind="Internal").ap()
    wgq_scr = nc.dram_tensor("wgq_scr", [I, D], BF16, kind="Internal").ap()
    wuq_scr = nc.dram_tensor("wuq_scr", [I, D], BF16, kind="Internal").ap()
    wdq_scr = nc.dram_tensor("wdq_scr", [D, I], BF16, kind="Internal").ap()


    sch = Sched()
    S = sch.step
    B = lambda n: Buf(n)

    es0 = ExitStack()
    sbt0 = lambda n, s, d: es0.enter_context(nc.sbuf_tensor(n, s, d))
    idn = sbt0("idn", [128, 128], BF16)
    b_idn = B("idn")
    b_xscr, b_zscr = B("xscr"), B("zscr")
    b_wqscr = {"g": B("wgscr"), "u": B("wuscr"), "d": B("wdscr")}
    b_out = B("out")

    with es0:
        sems = {}
        sem_es = ExitStack()
        with sem_es:
            for e in ENGS:
                sems[e] = sem_es.enter_context(nc.semaphore(f"sem_{e}"))

            es1 = ExitStack()
            sbt = lambda n, s, d: es1.enter_context(nc.sbuf_tensor(n, s, d))
            pst = lambda n, s, d: es1.enter_context(nc.psum_tensor(n, s, d))

            wk_sb = sbt("wk_sb", [128, DC * D], BF16)
            wv_sb = sbt("wv_sb", [128, DC * D], BF16)
            wr_sb = sbt("wr_sb", [128, DC * D], BF16)
            wo_sb = sbt("wo_sb", [128, DC * D], BF16)
            tdc = sbt("tdc", [128, DC], F32)
            ewc = sbt("ewc", [128, DC], F32)
            ufc = sbt("ufc", [128, DC], F32)
            tmkc = sbt("tmkc", [128, DC], F32)
            tmvc = sbt("tmvc", [128, DC], F32)
            tmrc = sbt("tmrc", [128, DC], F32)
            zrow = sbt("zrow", [1, D], BF16)

            xa = [sbt(f"xa{j}", [128, DC * TC], BF16) for j in range(3)]
            xb = [sbt(f"xb{j}", [128, DC * TC], BF16) for j in range(3)]
            _dxb = sbt("dxb", [128, DC * TC], BF16)
            dxb = [_dxb, _dxb]
            mixk = [sbt(f"mixk{j}", [128, DC * TC], BF16) for j in range(2)]
            mixv = [sbt(f"mixv{j}", [128, DC * TC], BF16) for j in range(2)]
            mixr = [sbt(f"mixr{j}", [128, DC * TC], BF16) for j in range(2)]
            _mtmp = sbt("mtmp", [128, DC * TC], BF16)
            mtmp = [_mtmp, _mtmp]
            ek_t = sbt("ek_t", [128, DC * TC], BF16)
            euk_t = sbt("euk_t", [128, DC * TC], BF16)
            vv_t = sbt("vv_t", [128, DC * TC], BF16)
            ekv_t = sbt("ekv_t", [128, DC * TC], BF16)
            eukv_t = sbt("eukv_t", [128, DC * TC], BF16)
            sr_t = sbt("sr_t", [128, DC * TC], BF16)
            er_t = [sbt(f"er_t{j}", [128, TC], BF16) for j in range(2)]
            l1_t = [sbt(f"l1_t{j}", [128, TC], F32) for j in range(2)]
            rwkv_t = [sbt(f"rwkv{j}", [128, DC * TC], BF16) for j in range(2)]
            _z16 = sbt("z16", [128, DC * TC], BF16)
            z16_t = [_z16, _z16]
            alfp = sbt("alfp", [128, 2 * (TC + 1)], F32)
            betp = sbt("betp", [128, 2 * (TC + 1)], F32)
            alf = [alfp[:, 0:TC + 1], alfp[:, TC + 1:2 * (TC + 1)]]
            bet = [betp[:, 0:TC + 1], betp[:, TC + 1:2 * (TC + 1)]]
            numt = [sbt(f"numt{j}", [128, 2 * TC], F32) for j in range(2)]
            dent = [sbt(f"dent{j}", [128, 2 * TC], F32) for j in range(2)]
            rden = [sbt(f"rden{j}", [128, 2 * TC], BF16) for j in range(2)]
            wkvt = [sbt(f"wkvt{j}", [128, 2 * TC], BF16) for j in range(2)]
            acar = sbt("acar", [128, DC], F32)
            bcar = sbt("bcar", [128, DC], F32)

            WQW = WQW_G
            wload = [sbt(f"wload{j}", [128, WQW], F32) for j in range(2)]
            _wqa = sbt("wq_a", [128, WQW], F32)
            wq_a = [_wqa, _wqa]
            wq_t = [sbt(f"wq_t{j}", [128, WQW], F32) for j in range(2)]
            _wqc3 = [sbt(f"wq_c{k}", [128, WQW], BF16) for k in range(3)]
            wq_f = [sbt(f"wq_f{j}", [128, WQW], BF16) for j in range(2)]
            wq_c = [_wqc3 + [wq_f[0]], _wqc3 + [wq_f[1]]]
            _wqsg = sbt("wq_sg", [128, WQW], BF16)
            wq_sg = [_wqsg, _wqsg]
            wq_bm = [sbt(f"wq_bm{j}", [128, WQW // 16], F32) for j in range(2)]
            wq_rs = [sbt(f"wq_rs{j}", [128, 2 * (WQW // 16)], F32) for j in range(2)]
            wq_o = [sbt(f"wq_o{j}", [128, WQW], BF16) for j in range(2)]

            ps_k = [pst(f"ps_k{j}", [128, TC], F32) for j in range(2)]
            ps_v = [pst(f"ps_v{j}", [128, TC], F32) for j in range(2)]
            ps_r = [pst(f"ps_r{j}", [128, TC], F32) for j in range(2)]
            ps_at = [pst(f"ps_at{j}", [128, TC], F32) for j in range(2)]

            b_wk, b_wv, b_wr, b_wo = B("wk"), B("wv"), B("wr"), B("wo")
            b_tdc, b_ewc, b_ufc = B("tdc"), B("ewc"), B("ufc")
            b_tmk, b_tmv, b_tmr = B("tmk"), B("tmv"), B("tmr")
            b_zrow = B("zrow")
            b_xa = [B(f"xa{j}") for j in range(3)]
            b_xb = [B(f"xb{j}") for j in range(3)]
            _bdxb = B("dxb")
            b_dxb = [_bdxb, _bdxb]
            b_mixk = [B(f"mixk{j}") for j in range(2)]
            b_mixv = [B(f"mixv{j}") for j in range(2)]
            b_mixr = [B(f"mixr{j}") for j in range(2)]
            b_mtmp4 = [B(f"mtmp{j}") for j in range(4)]
            b_ek, b_euk, b_vv = B("ek"), B("euk"), B("vv")
            b_ekv, b_eukv = B("ekv"), B("eukv")
            b_sr = B("sr")
            b_er = [B(f"er{j}") for j in range(2)]
            b_l1 = [B(f"l1{j}") for j in range(2)]
            b_rwkv = [B(f"rwkv{j}") for j in range(2)]
            _bz16 = B("z16")
            b_z16 = [_bz16, _bz16]
            b_alf = [B(f"alf{j}") for j in range(2)]
            b_bet = [B(f"bet{j}") for j in range(2)]
            b_num = [B(f"num{j}") for j in range(2)]
            b_den = [B(f"den{j}") for j in range(2)]
            b_rden = [B(f"rden{j}") for j in range(2)]
            b_wkv = [B(f"wkv{j}") for j in range(2)]
            b_acar, b_bcar = B("acar"), B("bcar")
            b_psk = [B(f"psk{j}") for j in range(2)]
            b_psv = [B(f"psv{j}") for j in range(2)]
            b_psr = [B(f"psr{j}") for j in range(2)]
            b_psat = [B(f"psat{j}") for j in range(2)]
            b_wload = [B(f"wload{j}") for j in range(2)]
            _bwqa = B("wqa")
            b_wqa = [_bwqa, _bwqa]
            b_wqt = [B(f"wqt{j}") for j in range(2)]
            _bwqsg = B("wqsg")
            b_wqsg = [_bwqsg, _bwqsg]
            _bwqc3 = [B(f"wqc{k}") for k in range(3)]
            b_wqc = [_bwqc3 + [B("wqf0")], _bwqc3 + [B("wqf1")]]
            b_wqbm = [B(f"wqbm{j}") for j in range(2)]
            b_wqrs = [B(f"wqrs{j}") for j in range(2)]
            b_wqo = [B(f"wqo{j}") for j in range(2)]

            for (dst, buf, src) in ((tdc, b_tdc, td2), (ufc, b_ufc, tf2),
                                    (tmkc, b_tmk, tmk2), (tmvc, b_tmv, tmv2),
                                    (tmrc, b_tmr, tmr2), (idn, b_idn, ident_in)):
                S("sp", lambda e, dst=dst, src=src: e.dma_start(out=dst[:, :], in_=src),
                  writes=[buf], dma=True)
            S("act", lambda e: e.activation(ewc[:, :], tdc[:, :], AF.Exp),
              reads=[b_tdc], writes=[b_ewc])
            S("act", lambda e: e.activation(ewc[:, :], ewc[:, :], AF.Exp, scale=-1.0),
              reads=[b_ewc], writes=[b_ewc])
            S("dve", lambda e: e.memset(zrow[:, :], 0.0), writes=[b_zrow])
            b_xz = [B(f"xz{s}") for s in range(SEQ)]
            for s in range(SEQ):
                S("sp", lambda e, s=s: e.dma_start(out=xscr[s, 0:1, :], in_=zrow[:, :]),
                  reads=[b_zrow], writes=[b_xz[s]], dma=True)
            NCH0 = T // TC
            b_xch = [[B(f"xch{s}_{c}") for c in range(NCH0)] for s in range(SEQ)]
            xch_done = set()

            def emit_xcast(s, c):
                if (s, c) in xch_done or c >= NCH0:
                    return
                xch_done.add((s, c))
                t0 = c * TC
                S("pool", lambda e, s=s, t0=t0: e.dma_start(
                    out=xscr[s, 1 + t0:1 + t0 + TC, :],
                    in_=x_in[s * T + t0:s * T + t0 + TC, :]),
                  writes=[b_xch[s][c]], dma=True)

            emit_xcast(0, 0)
            emit_xcast(0, 1)
            for (dst, buf, src) in ((wk_sb, b_wk, wkT), (wv_sb, b_wv, wvT),
                                    (wr_sb, b_wr, wrT), (wo_sb, b_wo, woT)):
                for dc in range(DC):
                    S("pool", lambda e, dst=dst, src=src, dc=dc: e.dma_start(
                        out=dst[:, dc * D:(dc + 1) * D],
                        in_=src[dc * 128:(dc + 1) * 128, :]),
                      writes=[buf], dma=True)

            wq_units = wq_unit_list()
            scr_map = {"g": wgq_scr, "u": wuq_scr, "d": wdq_scr}
            src_map = {"g": wg_in, "u": wu_in, "d": wd_in}

            def emit_quant_front(src_buf, src_ap, w,
                                 a_t, a_b, t_t, t_b, sg_t, sg_b,
                                 bm_t, bm_b, rs_t, rs_b, f_t, f_b):
                nb = w // 16
                aT, tT = a_t[:, 0:w], t_t[:, 0:w]
                fT = f_t[:, 0:w]
                sgT = sg_t[:, 0:w]
                bmT = bm_t[:, 0:nb]
                rsT = rs_t[:, 0:nb]
                ssT = rs_t[:, nb:2 * nb]
                S("act", lambda e: e.activation(aT, src_ap, AF.Abs),
                  reads=[src_buf], writes=[a_b])
                S("act", lambda e: e.activation(sgT, src_ap, AF.Sign),
                  reads=[src_buf], writes=[sg_b])
                S("dve", lambda e: e.reduce_max(
                    bmT, src_ap.rearrange("p (b w) -> p b w", w=16),
                    AX.X, apply_absolute_value=True),
                  reads=[src_buf], writes=[bm_b])
                S("dve", lambda e: e.tensor_scalar(rsT, bmT, 6e-12, None, ALU.max),
                  reads=[bm_b], writes=[rs_b])
                S("dve", lambda e: e.reciprocal(rsT, rsT), reads=[rs_b], writes=[rs_b])
                S("dve", lambda e: e.tensor_scalar(rsT, rsT, 6.0, None, ALU.mult),
                  reads=[rs_b], writes=[rs_b])
                S("dve", lambda e: e.tensor_scalar(ssT, bmT, 1.0 / 6.0, 1e-12,
                                                   ALU.mult, ALU.max),
                  reads=[bm_b], writes=[rs_b])
                rs_bc = rs_t[:, 0:nb].unsqueeze(2).broadcast_to([128, nb, 16])
                s_bc = rs_t[:, nb:2 * nb].unsqueeze(2).broadcast_to([128, nb, 16])
                S("pool", lambda e: e.tensor_tensor(tT, aT, rs_bc, ALU.mult),
                  reads=[a_b, rs_b], writes=[t_b])
                S("pool", lambda e: e.tensor_tensor(fT, sgT, s_bc, ALU.mult),
                  reads=[sg_b, rs_b], writes=[f_b])

            def emit_quant_back(w, t_t, t_b, c_ts, c_bs, outs, out_buf):
                tT = t_t[:, 0:w]
                c0, c1, c2 = (c_ts[0][:, 0:w], c_ts[1][:, 0:w], c_ts[2][:, 0:w])
                fT = c_ts[3][:, 0:w]
                S("dve", lambda e: e.tensor_scalar(c0, tT, QTH[0], QDL[0], ALU.is_gt, ALU.mult),
                  reads=[t_b], writes=[c_bs[0]])
                S("dve", lambda e: e.tensor_scalar(c1, tT, QTH[1], QDL[1], ALU.is_gt, ALU.mult),
                  reads=[t_b], writes=[c_bs[1]])
                S("dve", lambda e: e.tensor_tensor(c0, c0, c1, ALU.add),
                  reads=[c_bs[0], c_bs[1]], writes=[c_bs[0]])
                S("dve", lambda e: e.tensor_scalar(c1, tT, QTH[2], QDL[2], ALU.is_gt, ALU.mult),
                  reads=[t_b], writes=[c_bs[1]])
                S("dve", lambda e: e.tensor_scalar(c2, tT, QTH[3], QDL[3], ALU.is_gt, ALU.mult),
                  reads=[t_b], writes=[c_bs[2]])
                S("dve", lambda e: e.tensor_tensor(c1, c1, c2, ALU.add),
                  reads=[c_bs[1], c_bs[2]], writes=[c_bs[1]])
                S("dve", lambda e: e.tensor_tensor(c0, c0, c1, ALU.add),
                  reads=[c_bs[0], c_bs[1]], writes=[c_bs[0]])
                S("dve", lambda e: e.tensor_scalar(c1, tT, QTH[4], QDL[4], ALU.is_gt, ALU.mult),
                  reads=[t_b], writes=[c_bs[1]])
                S("dve", lambda e: e.tensor_scalar(c2, tT, QTH[5], QDL[5], ALU.is_gt, ALU.mult),
                  reads=[t_b], writes=[c_bs[2]])
                S("dve", lambda e: e.tensor_tensor(c1, c1, c2, ALU.add),
                  reads=[c_bs[1], c_bs[2]], writes=[c_bs[1]])
                S("dve", lambda e: e.tensor_scalar(c2, tT, QTH[6], QDL[6], ALU.is_gt, ALU.mult),
                  reads=[t_b], writes=[c_bs[2]])
                S("dve", lambda e: e.tensor_tensor(c1, c1, c2, ALU.add),
                  reads=[c_bs[1], c_bs[2]], writes=[c_bs[1]])
                S("dve", lambda e: e.tensor_tensor(c0, c0, c1, ALU.add),
                  reads=[c_bs[0], c_bs[1]], writes=[c_bs[0]])
                S("dve", lambda e: e.tensor_tensor(outs, c0, fT, ALU.mult),
                  reads=[c_bs[0], c_bs[3]], writes=[out_buf])

            def emit_quant(src_buf, src_ap, w, outs, out_buf,
                           a_t, a_b, t_t, t_b, c_ts, c_bs, sg_t, sg_b,
                           bm_t, bm_b, rs_t, rs_b):
                emit_quant_front(src_buf, src_ap, w, a_t, a_b, t_t, t_b,
                                 sg_t, sg_b, bm_t, bm_b, rs_t, rs_b,
                                 c_ts[3], c_bs[3])
                emit_quant_back(w, t_t, t_b, c_ts, c_bs, outs, out_buf)

            wq_pending_store = []
            wq_pending_back = []
            wq_loaded = [0]

            def emit_wq_load(j):
                which, row0, col0, w = wq_units[j]
                jb = j % 2
                S("sp", lambda e, which=which, row0=row0, col0=col0, w=w, jb=jb:
                  e.dma_start(out=wload[jb][:, 0:w],
                              in_=src_map[which][row0:row0 + 128, col0:col0 + w]),
                  writes=[b_wload[jb]], dma=True)

            def emit_wq_unit(j):
                while wq_loaded[0] <= min(j + 1, len(wq_units) - 1):
                    emit_wq_load(wq_loaded[0])
                    wq_loaded[0] += 1
                which, row0, col0, w = wq_units[j]
                jb = j % 2
                lw = wload[jb][:, 0:w]
                emit_quant_front(b_wload[jb], lw, w,
                                 wq_a[jb], b_wqa[jb], wq_t[jb], b_wqt[jb],
                                 wq_sg[jb], b_wqsg[jb], wq_bm[jb], b_wqbm[jb],
                                 wq_rs[jb], b_wqrs[jb], wq_c[jb][3], b_wqc[jb][3])
                while wq_pending_back:
                    wq_pending_back.pop(0)()
                while wq_pending_store:
                    wq_pending_store.pop(0)()

                def back(which=which, row0=row0, col0=col0, w=w, jb=jb):
                    emit_quant_back(w, wq_t[jb], b_wqt[jb], wq_c[jb], b_wqc[jb],
                                    wq_o[jb][:, 0:w], b_wqo[jb])
                    wq_pending_store.append(
                        lambda: S("sp", lambda e: e.dma_start(
                            out=scr_map[which][row0:row0 + 128, col0:col0 + w],
                            in_=wq_o[jb][:, 0:w]),
                            reads=[b_wqo[jb]], writes=[b_wqscr[which]], dma=True))
                wq_pending_back.append(back)

            NCH = T // TC
            wq_next = [0]

            def emit_xloads(it):
                if it >= NI:
                    return
                s = it // NCH
                c = it % NCH
                t0 = c * TC
                ib = it % 3
                emit_xcast(s, c)
                xa_rd = [b_xch[s][c]]
                xb_rd = [b_xch[s][c], b_xch[s][c - 1] if c > 0 else b_xz[s]]
                for dc in range(DC):
                    S("sp", lambda e, s=s, t0=t0, dc=dc, ib=ib: e.dma_start(
                        out=xa[ib][:, dc * TC:(dc + 1) * TC],
                        in_=xscr[s, 1 + t0:1 + t0 + TC, dc * 128:(dc + 1) * 128],
                        transpose=True),
                      reads=xa_rd, writes=[b_xa[ib]], dma=True)
                    S("sp", lambda e, s=s, t0=t0, dc=dc, ib=ib: e.dma_start(
                        out=xb[ib][:, dc * TC:(dc + 1) * TC],
                        in_=xscr[s, t0:t0 + TC, dc * 128:(dc + 1) * 128],
                        transpose=True),
                      reads=xb_rd, writes=[b_xb[ib]], dma=True)

            def emit_mixes(it):
                if it >= NI:
                    return
                ib = it % 3
                mb = it % 2
                S("dve", lambda e, ib=ib, mb=mb: e.tensor_tensor(
                    dxb[mb][:, :], xa[ib][:, :], xb[ib][:, :], ALU.subtract),
                  reads=[b_xa[ib], b_xb[ib]], writes=[b_dxb[mb]])
                step = [0]
                for (mx, bmx, tmt, btm) in ((mixk[mb], b_mixk[mb], tmkc, b_tmk),
                                            (mixv[mb], b_mixv[mb], tmvc, b_tmv),
                                            (mixr[mb], b_mixr[mb], tmrc, b_tmr)):
                    for dc in range(DC):
                        sl = slice(dc * TC, (dc + 1) * TC)
                        bm4 = b_mtmp4[step[0] % 4]
                        step[0] += 1
                        S("dve", lambda e, sl=sl, tmt=tmt, mb=mb, dc=dc: e.tensor_scalar(
                            mtmp[mb][:, sl], dxb[mb][:, sl], tmt[:, dc:dc + 1],
                            None, ALU.mult),
                          reads=[b_dxb[mb], btm], writes=[bm4])
                        S("pool", lambda e, sl=sl, mx=mx, ib=ib, mb=mb: e.tensor_tensor(
                            mx[:, sl], mtmp[mb][:, sl], xb[ib][:, sl], ALU.add),
                          reads=[bm4, b_xb[ib]], writes=[bmx])

            def emit_attn(it):
                if it < 0:
                    return
                s = it // NCH
                gt0 = s * T + (it % NCH) * TC
                ib = it % 3
                rb = it % 2
                zb = it % 2
                for fc in range(DC):
                    pb = fc % 2
                    for ec in range(DC):
                        S("pe", lambda e, fc=fc, ec=ec, pb=pb, rb=rb: e.matmul(
                            ps_at[pb][:, :],
                            wo_sb[:, ec * D + fc * 128: ec * D + (fc + 1) * 128],
                            rwkv_t[rb][:, ec * TC:(ec + 1) * TC],
                            start=(ec == 0), stop=(ec == DC - 1)),
                          reads=[b_wo, b_rwkv[rb]], writes=[b_psat[pb]])
                    sl = slice(fc * TC, (fc + 1) * TC)
                    S("dve", lambda e, sl=sl, pb=pb, ib=ib, zb=zb: e.tensor_tensor(
                        z16_t[zb][:, sl], xa[ib][:, sl], ps_at[pb][:, :], ALU.add),
                      reads=[b_xa[ib], b_psat[pb]], writes=[b_z16[zb]])
                S("sp", lambda e, it=it, zb=zb: e.dma_start(
                    out=zscr[it, :, :], in_=z16_t[zb][:, :]),
                  reads=[b_z16[zb]], writes=[b_zscr], dma=True)

            while wq_loaded[0] < 2:
                emit_wq_load(wq_loaded[0])
                wq_loaded[0] += 1
            emit_xloads(0)
            emit_xloads(1)
            emit_mixes(0)
            wkv_pend = []
            for it in range(NI):
                c = it % NCH
                mb = it % 2
                while wkv_pend:
                    wkv_pend.pop(0)()
                while wq_loaded[0] <= min(wq_next[0] + 1, len(wq_units) - 1):
                    emit_wq_load(wq_loaded[0])
                    wq_loaded[0] += 1
                for (mix, bmix, wsb, bwsb, kind) in (
                        (mixk[mb], b_mixk[mb], wk_sb, b_wk, "k"),
                        (mixv[mb], b_mixv[mb], wv_sb, b_wv, "v"),
                        (mixr[mb], b_mixr[mb], wr_sb, b_wr, "r")):
                    for ec in range(DC):
                        pb = ec % 2
                        pss, bpss = {"k": (ps_k, b_psk), "v": (ps_v, b_psv),
                                     "r": (ps_r, b_psr)}[kind]
                        for dc in range(DC):
                            S("pe", lambda e, ec=ec, dc=dc, wsb=wsb, mix=mix, pb=pb,
                              pss=pss: e.matmul(
                                pss[pb][:, :],
                                wsb[:, dc * D + ec * 128: dc * D + (ec + 1) * 128],
                                mix[:, dc * TC:(dc + 1) * TC],
                                start=(dc == 0), stop=(dc == DC - 1)),
                              reads=[bwsb, bmix], writes=[bpss[pb]])
                        sl = slice(ec * TC, (ec + 1) * TC)
                        if kind == "k":
                            S("act", lambda e, sl=sl, pb=pb: e.activation(
                                ek_t[:, sl], ps_k[pb][:, :], AF.Exp),
                              reads=[b_psk[pb]], writes=[b_ek])
                            S("act", lambda e, sl=sl, pb=pb, ec=ec: e.activation(
                                euk_t[:, sl], ps_k[pb][:, :], AF.Exp,
                                bias=ufc[:, ec:ec + 1]),
                              reads=[b_psk[pb], b_ufc], writes=[b_euk])
                        elif kind == "v":
                            S("act", lambda e, sl=sl, pb=pb: e.activation(
                                vv_t[:, sl], ps_v[pb][:, :], AF.Copy),
                              reads=[b_psv[pb]], writes=[b_vv])
                        else:
                            eb = ec % 2
                            S("act", lambda e, pb=pb, eb=eb: e.activation(
                                er_t[eb][:, :], ps_r[pb][:, :], AF.Exp, scale=-1.0),
                              reads=[b_psr[pb]], writes=[b_er[eb]])
                            S("act", lambda e, eb=eb: e.activation(
                                l1_t[eb][:, :], er_t[eb][:, :], AF.Ln, bias=1.0),
                              reads=[b_er[eb]], writes=[b_l1[eb]])
                            S("act", lambda e, sl=sl, eb=eb: e.activation(
                                sr_t[:, sl], l1_t[eb][:, :], AF.Exp, scale=-1.0),
                              reads=[b_l1[eb]], writes=[b_sr])
                emit_mixes(it + 1)
                share = ceil_div(len(wq_units) - wq_next[0], NI - it)
                done = 0
                for _ in range(min(3, share)):
                    if wq_next[0] < len(wq_units):
                        emit_wq_unit(wq_next[0])
                        wq_next[0] += 1
                        done += 1
                emit_attn(it - 1)
                for _ in range(share - done):
                    if wq_next[0] < len(wq_units):
                        emit_wq_unit(wq_next[0])
                        wq_next[0] += 1
                emit_xloads(it + 2)
                S("dve", lambda e: e.tensor_tensor(ekv_t[:, :], ek_t[:, :], vv_t[:, :],
                                                   ALU.mult),
                  reads=[b_ek, b_vv], writes=[b_ekv])
                S("dve", lambda e: e.tensor_tensor(eukv_t[:, :], euk_t[:, :], vv_t[:, :],
                                                   ALU.mult),
                  reads=[b_euk, b_vv], writes=[b_eukv])
                rb = it % 2
                for dc in range(DC):
                    ab = dc % 2
                    sl = slice(dc * TC, (dc + 1) * TC)
                    if c == 0:
                        S("dve", lambda e, dc=dc: e.memset(acar[:, dc:dc + 1], 0.0),
                          writes=[b_acar])
                        S("dve", lambda e, dc=dc: e.memset(bcar[:, dc:dc + 1], 0.0),
                          writes=[b_bcar])
                    S("dve", lambda e, dc=dc, ab=ab: e.tensor_copy(
                        alf[ab][:, 0:1], acar[:, dc:dc + 1]),
                      reads=[b_acar], writes=[b_alf[ab]])
                    S("dve", lambda e, dc=dc, ab=ab, sl=sl: e.tensor_tensor_scan(
                        alf[ab][:, 1:TC + 1], ewc[:, dc:dc + 1].broadcast_to([128, TC]),
                        ekv_t[:, sl], acar[:, dc:dc + 1], ALU.mult, ALU.add),
                      reads=[b_ewc, b_ekv, b_acar], writes=[b_alf[ab]])
                    S("dve", lambda e, dc=dc, ab=ab: e.tensor_copy(
                        acar[:, dc:dc + 1], alf[ab][:, TC:TC + 1]),
                      reads=[b_alf[ab]], writes=[b_acar])
                    S("dve", lambda e, dc=dc, ab=ab: e.tensor_copy(
                        bet[ab][:, 0:1], bcar[:, dc:dc + 1]),
                      reads=[b_bcar], writes=[b_bet[ab]])
                    S("dve", lambda e, dc=dc, ab=ab, sl=sl: e.tensor_tensor_scan(
                        bet[ab][:, 1:TC + 1], ewc[:, dc:dc + 1].broadcast_to([128, TC]),
                        ek_t[:, sl], bcar[:, dc:dc + 1], ALU.mult, ALU.add),
                      reads=[b_ewc, b_ek, b_bcar], writes=[b_bet[ab]])
                    S("dve", lambda e, dc=dc, ab=ab: e.tensor_copy(
                        bcar[:, dc:dc + 1], bet[ab][:, TC:TC + 1]),
                      reads=[b_bet[ab]], writes=[b_bcar])
                    if ab == 1:
                        if len(wkv_pend) > 1:
                            wkv_pend.pop(0)()
                        pr = (dc // 2) % 2
                        sl2 = slice((dc - 1) * TC, (dc + 1) * TC)
                        a_sh = alfp.rearrange("p (s c) -> p s c", s=2)[:, :, 0:TC]
                        b_sh = betp.rearrange("p (s c) -> p s c", s=2)[:, :, 0:TC]
                        S("dve", lambda e, pr=pr, sl2=sl2, a_sh=a_sh: e.tensor_tensor(
                            numt[pr][:, :], a_sh, eukv_t[:, sl2], ALU.add),
                          reads=[b_alf[0], b_alf[1], b_eukv], writes=[b_num[pr]])
                        S("dve", lambda e, pr=pr, sl2=sl2, b_sh=b_sh: e.tensor_tensor(
                            dent[pr][:, :], b_sh, euk_t[:, sl2], ALU.add),
                          reads=[b_bet[0], b_bet[1], b_euk], writes=[b_den[pr]])
                        S("act", lambda e, pr=pr: e.activation(
                            dent[pr][:, :], dent[pr][:, :], AF.Ln),
                          reads=[b_den[pr]], writes=[b_den[pr]])
                        S("act", lambda e, pr=pr: e.activation(
                            rden[pr][:, :], dent[pr][:, :], AF.Exp, scale=-1.0),
                          reads=[b_den[pr]], writes=[b_rden[pr]])
                        def wkv_fin(pr=pr, sl2=sl2, rb=rb):
                            S("dve", lambda e: e.tensor_tensor(
                                wkvt[pr][:, :], numt[pr][:, :], rden[pr][:, :],
                                ALU.mult),
                              reads=[b_num[pr], b_rden[pr]], writes=[b_wkv[pr]])
                            S("dve", lambda e: e.tensor_tensor(
                                rwkv_t[rb][:, sl2], wkvt[pr][:, :], sr_t[:, sl2],
                                ALU.mult),
                              reads=[b_wkv[pr], b_sr], writes=[b_rwkv[rb]])
                        wkv_pend.append(wkv_fin)
            while wkv_pend:
                wkv_pend.pop(0)()
            emit_attn(NI - 1)
            while wq_next[0] < len(wq_units):
                emit_wq_unit(wq_next[0])
                wq_next[0] += 1
            while wq_pending_back:
                wq_pending_back.pop(0)()
            while wq_pending_store:
                wq_pending_store.pop(0)()

            phase1_end = {e: len(sch.items[e]) for e in ENGS}

            for sname in sch.sem_names:
                if sname not in sems:
                    sems[sname] = sem_es.enter_context(nc.semaphore(sname))

            with nc.Block() as blk1:
                def mk_body(ename, lo, hi):
                    def body(eng):
                        seen = {}
                        for fn, waits, inc in sch.items[ename][lo:hi]:
                            for we, wv in sorted(waits.items()):
                                if seen.get(we, -1) < wv:
                                    eng.wait_ge(sems[we], wv)
                                    seen[we] = wv
                            fn(eng).then_inc(sems[inc[0]], inc[1])
                    return body
                blk1.sync(mk_body("sp", 0, phase1_end["sp"]))
                blk1.tensor(mk_body("pe", 0, phase1_end["pe"]))
                blk1.scalar(mk_body("act", 0, phase1_end["act"]))
                blk1.vector(mk_body("dve", 0, phase1_end["dve"]))
                blk1.gpsimd(mk_body("pool", 0, phase1_end["pool"]))
            es1.close()

            sch.fence_all()
            es2 = ExitStack()
            sbt = lambda n, s, d: es2.enter_context(nc.sbuf_tensor(n, s, d))
            pst = lambda n, s, d: es2.enter_context(nc.psum_tensor(n, s, d))

            wgqT = sbt("wgqT", [128, DC * I], BF16)
            wuqT = sbt("wuqT", [128, DC * I], BF16)
            wdqT = sbt("wdqT", [128, (I // 128) * D], BF16)
            z16t = [sbt(f"z16t{j}", [128, D], BF16) for j in range(2)]
            HW2 = 1024
            _ha = sbt("h_a", [128, HW2], BF16)
            h_a = [_ha, _ha]
            _htq = sbt("h_tq", [128, HW2], BF16)
            h_tq = [_htq, _htq]
            _hc = [sbt(f"h_c{k}", [128, HW2], BF16) for k in range(3)]
            h_f = [sbt(f"h_f{j}", [128, HW2], BF16) for j in range(2)]
            h_c = [_hc + [h_f[0]], _hc + [h_f[1]]]
            _hsg = sbt("h_sg", [128, HW2], BF16)
            h_sg = [_hsg, _hsg]
            h_bm = [sbt(f"h_bm{j}", [128, HW2 // 16], F32) for j in range(2)]
            h_rs = [sbt(f"h_rs{j}", [128, 2 * (HW2 // 16)], F32) for j in range(2)]
            yq = sbt("yq", [128, D], BF16)
            yqT = [sbt(f"yqT{j}", [128, D], BF16) for j in range(2)]
            em_t = [sbt(f"em_t{j}", [128, 512], BF16) for j in range(2)]
            _el = sbt("el_t", [128, 512], F32)
            el_t = [_el, _el]
            h_t = [sbt(f"h_t{j}", [128, HW2], BF16) for j in range(2)]
            hq = [sbt(f"hq{j}", [128, HW2 if j < 2 else 768], BF16)
                  for j in range(3)]
            hqT = [sbt(f"hqT{j}", [128, HW2 if j < 2 else 768], BF16)
                   for j in range(3)]
            z2 = sbt("z2", [128, D], F32)
            y32 = [sbt(f"y32_{j}", [128, D], F32) for j in range(3)]
            o32 = [sbt(f"o32_{j}", [128, D], F32) for j in range(2)]
            st1 = [sbt(f"st1_{j}", [128, 16], F32) for j in range(2)]
            qt2 = sbt("q_t2", [128, HW2], BF16)

            qts = [h_tq[0], qt2]

            ps_tr = [pst(f"ps_tr{j}", [128, D], BF16) for j in range(2)]
            ps_g = [pst(f"ps_g{j}", [128, 512], F32) for j in range(2)]
            ps_u = [pst(f"ps_u{j}", [128, 512], F32) for j in range(2)]
            ps_dn = pst("ps_dn", [128, D], F32)

            b_wgqT, b_wuqT, b_wdqT = B("wgqT"), B("wuqT"), B("wdqT")
            b_z16t = [B(f"z16t{j}") for j in range(2)]
            _bha = B("ha")
            b_ha = [_bha, _bha]
            _bhtq = B("htq")
            b_htq = [_bhtq, _bhtq]
            _bhc = [B(f"hc{k}") for k in range(3)]
            b_hc = [_bhc + [B("hf0")], _bhc + [B("hf1")]]
            _bhsg = B("hsg")
            b_hsg = [_bhsg, _bhsg]
            b_hbm = [B(f"hbm{j}") for j in range(2)]
            b_hrs = [B(f"hrs{j}") for j in range(2)]
            b_yq = B("yq")
            b_yqT = [B(f"yqT{j}") for j in range(2)]
            b_em = [B(f"em{j}") for j in range(2)]
            _bel = B("el")
            b_el = [_bel, _bel]
            b_h = [B(f"h{j}") for j in range(2)]
            b_hq = [B(f"hq{j}") for j in range(3)]
            b_hqT = [B(f"hqT{j}") for j in range(3)]
            b_z2 = B("z2")
            b_y32 = [B(f"y32{j}") for j in range(3)]
            b_o32 = [B(f"o32{j}") for j in range(2)]
            b_st1L = [B(f"st1L{j}") for j in range(2)]
            b_st1H = [B(f"st1H{j}") for j in range(2)]
            b_qts = [b_htq[0], B("qt2")]
            b_pstr = [B("pstr0"), B("pstr1")]
            b_psg = [B(f"psg{j}") for j in range(2)]
            b_psu = [B(f"psu{j}") for j in range(2)]
            b_psdn = B("psdn")

            def emit_zload(ti):
                if ti >= NP2:
                    return
                g0 = ti * 128
                it_src = g0 // TC
                toff = g0 % TC
                zb = ti % 2
                for dc in range(DC):
                    S("sp", lambda e, dc=dc, it_src=it_src, toff=toff, zb=zb:
                      e.dma_start(
                        out=z16t[zb][:, dc * 128:(dc + 1) * 128],
                        in_=zscr[it_src, 0:128, dc * TC + toff:dc * TC + toff + 128],
                        transpose=True),
                      reads=[b_zscr], writes=[b_z16t[zb]], dma=True)

            emit_zload(0)
            emit_zload(1)
            for dc in range(DC):
                S("sp", lambda e, dc=dc: e.dma_start(
                    out=wgqT[:, dc * I:(dc + 1) * I],
                    in_=wgq_scr[:, dc * 128:(dc + 1) * 128], transpose=True),
                  reads=[b_wqscr["g"]], writes=[b_wgqT], dma=True)
            for dc in range(DC):
                S("sp", lambda e, dc=dc: e.dma_start(
                    out=wuqT[:, dc * I:(dc + 1) * I],
                    in_=wuq_scr[:, dc * 128:(dc + 1) * 128], transpose=True),
                  reads=[b_wqscr["u"]], writes=[b_wuqT], dma=True)
            for ic in range(I // 128):
                S("sp", lambda e, ic=ic: e.dma_start(
                    out=wdqT[:, ic * D:(ic + 1) * D],
                    in_=wdq_scr[:, ic * 128:(ic + 1) * 128], transpose=True),
                  reads=[b_wqscr["d"]], writes=[b_wdqT], dma=True)

            for j in range(2):
                S("dve", lambda e, j=j: e.memset(st1[j][:, 7:8], LN_EPS),
                  writes=[b_st1L[j]])
                S("dve", lambda e, j=j: e.memset(st1[j][:, 15:16], LN_EPS),
                  writes=[b_st1H[j]])

            def emit_ln(sb, base, bs1, z_ap, z_buf, o_ap, o_buf):
                s1 = st1[sb]
                b0 = base
                S("dve", lambda e: e.tensor_scalar(s1[:, b0 + 2:b0 + 4],
                                                   s1[:, b0:b0 + 2], 1.0 / D,
                                                   None, ALU.mult),
                  reads=[bs1], writes=[bs1])
                S("dve", lambda e: e.tensor_scalar(s1[:, b0 + 4:b0 + 5],
                                                   s1[:, b0 + 2:b0 + 3],
                                                   s1[:, b0 + 2:b0 + 3], None, ALU.mult),
                  reads=[bs1], writes=[bs1])
                S("dve", lambda e: e.tensor_tensor(s1[:, b0 + 4:b0 + 5],
                                                   s1[:, b0 + 3:b0 + 4],
                                                   s1[:, b0 + 4:b0 + 5], ALU.subtract),
                  reads=[bs1], writes=[bs1])
                S("act", lambda e: e.activation(s1[:, b0 + 5:b0 + 6],
                                                s1[:, b0 + 4:b0 + 5], AF.Ln,
                                                bias=s1[:, b0 + 7:b0 + 8]),
                  reads=[bs1], writes=[bs1])
                S("act", lambda e: e.activation(s1[:, b0 + 6:b0 + 7],
                                                s1[:, b0 + 5:b0 + 6], AF.Exp,
                                                scale=-0.5),
                  reads=[bs1], writes=[bs1])
                S("dve", lambda e: e.tensor_scalar(o_ap, z_ap, s1[:, b0 + 2:b0 + 3],
                                                   s1[:, b0 + 6:b0 + 7],
                                                   ALU.subtract, ALU.mult),
                  reads=[z_buf, bs1], writes=[o_buf])

            qsk = [0]
            qc_t = _hc
            qc_b = _bhc

            def qfront(src_buf, src_ap, w):
                par = qsk[0] % 2
                qsk[0] += 1
                emit_quant_front(src_buf, src_ap, w,
                                 h_a[0], b_ha[0], qts[par], b_qts[par],
                                 h_sg[0], b_hsg[0], h_bm[par], b_hbm[par],
                                 h_rs[par], b_hrs[par], h_f[par], b_hf(par))
                return par

            def b_hf(par):
                return b_hc[par][3]

            def qback(par, w, outs, out_buf):
                emit_quant_back(w, qts[par], b_qts[par],
                                qc_t + [h_f[par]], qc_b + [b_hf(par)],
                                outs, out_buf)

            def emit_stats(ti):
                p = ti % 2
                S("act", lambda e, p=p: e.activation(
                    z2[:, :], z16t[p][:, :], AF.Copy, accum_out=st1[p][:, 0:1]),
                  reads=[b_z16t[p]], writes=[b_z2, b_st1L[p]])
                S("act", lambda e, p=p: e.activation(
                    z2[:, :], z16t[p][:, :], AF.Square, accum_out=st1[p][:, 1:2]),
                  reads=[b_z16t[p]], writes=[b_z2, b_st1L[p]])

            def emit_yunit(ti):
                p = ti % 2
                p3 = ti % 3
                emit_ln(p, 0, b_st1L[p], z16t[p][:, :], b_z16t[p],
                        y32[p3][:, :], b_y32[p3])
                return qfront(b_y32[p3], y32[p3][:, :], D)

            def emit_yback(ti, par):
                qback(par, D, yq[:, :], b_yq)

            def emit_ytrans(ti):
                p = ti % 2
                for dc in range(DC):
                    S("pe", lambda e, dc=dc: e.transpose(
                        ps_tr[1][:, dc * 128:(dc + 1) * 128],
                        yq[:, dc * 128:(dc + 1) * 128], idn[:, :]),
                      reads=[b_yq, b_idn], writes=[b_pstr[1]])
                S("act", lambda e, p=p: e.activation(yqT[p][:, :], ps_tr[1][:, :],
                                                     AF.Copy),
                  reads=[b_pstr[1]], writes=[b_yqT[p]])

            def emit_piece_tail(p0, pw, dpi):
                nic = pw // 128
                pb = dpi % 2
                for j in range(nic):
                    S("pe", lambda e, j=j, dpi=dpi, pb=pb: e.transpose(
                        ps_tr[pb][:, j * 128:(j + 1) * 128],
                        hq[dpi][:, j * 128:(j + 1) * 128], idn[:, :]),
                      reads=[b_hq[dpi], b_idn], writes=[b_pstr[pb]])
                S("act", lambda e, nic=nic, dpi=dpi, pb=pb: e.activation(
                    hqT[dpi][:, 0:nic * 128], ps_tr[pb][:, 0:nic * 128], AF.Copy),
                  reads=[b_pstr[pb]], writes=[b_hqT[dpi]])
                for j in range(nic):
                    ic = (p0 // 128) + j
                    for nh in range(2):
                        S("pe", lambda e, j=j, ic=ic, nh=nh, dpi=dpi,
                          first=(p0 == 0 and j == 0): e.matmul(
                            ps_dn[:, nh * 512:(nh + 1) * 512],
                            hqT[dpi][:, j * 128:(j + 1) * 128],
                            wdqT[:, ic * D + nh * 512: ic * D + (nh + 1) * 512],
                            start=first, stop=(ic == I // 128 - 1)),
                          reads=[b_hqT[dpi], b_wdqT], writes=[b_psdn])

            pending_tails = []
            pt_ctr = [0]

            def emit_dpback(info):
                d0, dw, dpi, par, owner = info
                qback(par, dw, hq[dpi][:, 0:dw], b_hq[dpi])
                zd = owner if (d0 + dw == I) else None
                pending_tails.append(
                    (pt_ctr[0], lambda: emit_piece_tail(d0, dw, dpi), zd))

            def flush_tails_aged(final=False):
                zd = None
                while pending_tails and (final or
                                         pending_tails[0][0] <= pt_ctr[0] - 2):
                    _, fn, z = pending_tails.pop(0)
                    fn()
                    if z is not None:
                        zd = z
                pt_ctr[0] += 1
                return zd

            def emit_zunit(ti):
                p = ti % 2
                p3 = ti % 3
                gt0 = ti * 128
                S("dve", lambda e, p=p, p3=p3: e.scalar_tensor_tensor(
                    z2[:, :], ps_dn[:, :], 1.0, y32[p3][:, :], ALU.mult, ALU.add,
                    accum_out=st1[p][:, 8:9]),
                  reads=[b_psdn, b_y32[p3]], writes=[b_z2, b_st1H[p]])
                S("act", lambda e, p=p: e.activation(h_a[0][:, :], z2[:, :], AF.Square,
                                                     accum_out=st1[p][:, 9:10]),
                  reads=[b_z2], writes=[b_ha[0], b_st1H[p]])
                emit_ln(p, 8, b_st1H[p], z2[:, :], b_z2, o32[p][:, :], b_o32[p])
                S("act", lambda e, gt0=gt0, p=p: e.dma_start(
                    out=out[gt0:gt0 + 128, :], in_=o32[p][:, :]),
                  reads=[b_o32[p]], writes=[b_out], dma=True)

            dpcs = [(0, 1024), (1024, 1024), (2048, 768)]

            emit_stats(0)
            ypar = emit_yunit(0)
            emit_yback(0, ypar)
            emit_ytrans(0)

            dp2_pend = [None]
            for ti in range(NP2):
                yti = ti + 1
                zti = ti - 1
                if yti < NP2:
                    emit_stats(yti)
                emit_zload(ti + 2)
                if dp2_pend[0] is not None:
                    emit_dpback(dp2_pend.pop(0))
                    dp2_pend.append(None)
                if yti < NP2:
                    ypar = emit_yunit(yti)

                pending_dp = []
                for dpi, (d0, dw) in enumerate(dpcs):
                    dpb = dpi % 2
                    nhalf = ceil_div(dw, 512)
                    halves = []
                    for hf in range(nhalf):
                        p0 = d0 + hf * 512
                        pw = min(512, d0 + dw - p0)
                        hb = hf % 2
                        halves.append((hf, p0, pw, hb))
                        for dc in range(DC):
                            S("pe", lambda e, dc=dc, p0=p0, pw=pw, hb=hb, ti=ti:
                              e.matmul(
                                ps_g[hb][:, 0:pw],
                                yqT[ti % 2][:, dc * 128:(dc + 1) * 128],
                                wgqT[:, dc * I + p0: dc * I + p0 + pw],
                                start=(dc == 0), stop=(dc == DC - 1)),
                              reads=[b_yqT[ti % 2], b_wgqT], writes=[b_psg[hb]])
                        for dc in range(DC):
                            S("pe", lambda e, dc=dc, p0=p0, pw=pw, hb=hb, ti=ti:
                              e.matmul(
                                ps_u[hb][:, 0:pw],
                                yqT[ti % 2][:, dc * 128:(dc + 1) * 128],
                                wuqT[:, dc * I + p0: dc * I + p0 + pw],
                                start=(dc == 0), stop=(dc == DC - 1)),
                              reads=[b_yqT[ti % 2], b_wuqT], writes=[b_psu[hb]])
                    for hf, p0, pw, hb in halves:
                        S("act", lambda e, pw=pw, hb=hb: e.activation(
                            em_t[hb][:, 0:pw], ps_g[hb][:, 0:pw], AF.Exp, scale=-1.0),
                          reads=[b_psg[hb]], writes=[b_em[hb]])
                        S("act", lambda e, pw=pw, hb=hb: e.activation(
                            el_t[hb][:, 0:pw], em_t[hb][:, 0:pw], AF.Ln, bias=1.0),
                          reads=[b_em[hb]], writes=[b_el[hb]])
                        S("act", lambda e, pw=pw, hb=hb: e.activation(
                            em_t[hb][:, 0:pw], el_t[hb][:, 0:pw], AF.Exp, scale=-1.0),
                          reads=[b_el[hb]], writes=[b_em[hb]])
                        S("dve", lambda e, pw=pw, hb=hb: e.tensor_tensor(
                            em_t[hb][:, 0:pw], ps_u[hb][:, 0:pw], em_t[hb][:, 0:pw],
                            ALU.mult),
                          reads=[b_psu[hb], b_em[hb]], writes=[b_em[hb]])
                        S("dve", lambda e, pw=pw, hb=hb, hf=hf, dpb=dpb: e.tensor_tensor(
                            h_t[dpb][:, hf * 512: hf * 512 + pw],
                            em_t[hb][:, 0:pw], ps_g[hb][:, 0:pw], ALU.mult),
                          reads=[b_em[hb], b_psg[hb]], writes=[b_h[dpb]])
                    zd = flush_tails_aged()
                    if zd is not None:
                        emit_zunit(zd)
                    par = qfront(b_h[dpb], h_t[dpb][:, 0:dw], dw)
                    if pending_dp:
                        emit_dpback(pending_dp.pop(0))
                    elif yti < NP2:
                        emit_yback(yti, ypar)
                    pending_dp.append((d0, dw, dpi, par, ti))
                    if dpi == 1 and yti < NP2:
                        emit_ytrans(yti)
                dp2_pend[0] = pending_dp.pop(0)

            emit_dpback(dp2_pend.pop(0))
            zd = flush_tails_aged(final=True)
            if zd is not None:
                emit_zunit(zd)

            sch.step("sp", lambda e: e.nop(),
                     extra_deps=[("dma_out", 16 * b_out.dma_cnt["dma_out"])])

            for sname in sch.sem_names:
                if sname not in sems:
                    sems[sname] = sem_es.enter_context(nc.semaphore(sname))

            with nc.Block() as blk2:
                def mk_body2(ename):
                    lo = phase1_end[ename]

                    def body(eng):
                        seen = {}
                        for fn, waits, inc in sch.items[ename][lo:]:
                            for we, wv in sorted(waits.items()):
                                if seen.get(we, -1) < wv:
                                    eng.wait_ge(sems[we], wv)
                                    seen[we] = wv
                            fn(eng).then_inc(sems[inc[0]], inc[1])
                    return body
                blk2.sync(mk_body2("sp"))
                blk2.tensor(mk_body2("pe"))
                blk2.scalar(mk_body2("act"))
                blk2.vector(mk_body2("dve"))
                blk2.gpsimd(mk_body2("pool"))
            es2.close()
    return nc


def _prep_inputs(inputs):
    import ml_dtypes
    d = {}
    d["wkT"] = np.ascontiguousarray(inputs["L0_attn_Wk"].T)
    d["wvT"] = np.ascontiguousarray(inputs["L0_attn_Wv"].T)
    d["wrT"] = np.ascontiguousarray(inputs["L0_attn_Wr"].T)
    d["woT"] = np.ascontiguousarray(inputs["L0_attn_Wo"].T)
    d["wg"] = np.ascontiguousarray(inputs["L0_ffn_w_gate"], dtype=np.float32)
    d["wu"] = np.ascontiguousarray(inputs["L0_ffn_w_up"], dtype=np.float32)
    d["wd"] = np.ascontiguousarray(inputs["L0_ffn_w_down"], dtype=np.float32)
    d["td2"] = np.ascontiguousarray(
        np.asarray(inputs["L0_attn_time_decay"], dtype=np.float32).reshape(DC, 128).T)
    d["tf2"] = np.ascontiguousarray(
        np.asarray(inputs["L0_attn_time_first"], dtype=np.float32).reshape(DC, 128).T)
    d["tmk2"] = np.ascontiguousarray(
        np.asarray(inputs["L0_attn_time_mix_k"], dtype=np.float32).reshape(DC, 128).T)
    d["tmv2"] = np.ascontiguousarray(
        np.asarray(inputs["L0_attn_time_mix_v"], dtype=np.float32).reshape(DC, 128).T)
    d["tmr2"] = np.ascontiguousarray(
        np.asarray(inputs["L0_attn_time_mix_r"], dtype=np.float32).reshape(DC, 128).T)
    d["ident"] = np.eye(128, dtype=ml_dtypes.bfloat16)
    return d


def kernel(**inputs):
    inputs = {k: np.asarray(v) for k, v in inputs.items()}
    x = inputs["L0_attn_x"]
    Bfull, T, Dm = x.shape
    assert Dm == D and Bfull == SEQ * 8
    shared = _prep_inputs(inputs)
    nc = build(T=T, TC=256)
    in_maps = []
    for c in range(8):
        m = dict(shared)
        m["x"] = np.ascontiguousarray(
            x[c * SEQ:(c + 1) * SEQ].reshape(SEQ * T, D).astype(np.float32))
        in_maps.append(m)
    res = run_bass_kernel_spmd(nc, in_maps, core_ids=list(range(8)))
    outp = np.empty((Bfull, T, D), dtype=np.float32)
    for c in range(8):
        outp[c * SEQ:(c + 1) * SEQ] = np.asarray(res.results[c]["out"]).reshape(SEQ, T, D)
    return outp
